# version 10
# baseline (speedup 1.0000x reference)
import numpy as np
import ml_dtypes
from contextlib import ExitStack

import concourse.bass as bass
import concourse.mybir as mybir
from concourse.bass_utils import run_bass_kernel_spmd

F32 = mybir.dt.float32
BF16 = mybir.dt.bfloat16
I32 = mybir.dt.int32
AF = mybir.ActivationFunctionType
ALU = mybir.AluOpType
AX = mybir.AxisListType

D = 1024
HALF = 512
FF = 4096
T = 2048
WRM = 512
NW = WRM // 128
NT = T // 128
NG = 4
GAMMA = 0.96875
EPS_LN = 1e-5
EPS_Q = 1e-8
DEBUG = False
UPTO = "full"
_STAGES = ["warm", "attn", "f1", "f2a", "f2b", "g1", "g2"]


def _on(stage):
    return _STAGES.index(stage) <= _STAGES.index(UPTO) if UPTO != "full" else True


def pes_before(n):
    return n * (n + 1) // 2




def dma3(eng, out2d, in2d, n0, n1):
    return eng.dma_start(
        out=out2d.rearrange("p (a b) -> p a b", a=n0, b=n1),
        in_=in2d.rearrange("(a p) b -> p a b", p=128))

def build():
    nc = bass.Bass("TRN2", target_bir_lowering=False,
                   detect_race_conditions=False)

    x_own = nc.dram_tensor("x_own", [T, D], F32, kind="ExternalInput")
    x_warm = nc.dram_tensor("x_warm", [WRM, D], F32, kind="ExternalInput")
    wq_d = nc.dram_tensor("wq", [D, D], F32, kind="ExternalInput")
    wk_d = nc.dram_tensor("wk", [D, D], F32, kind="ExternalInput")
    wv_d = nc.dram_tensor("wv", [D, D], F32, kind="ExternalInput")
    w1_d = nc.dram_tensor("w1", [FF, D], F32, kind="ExternalInput")
    w2_d = nc.dram_tensor("w2", [D, FF], F32, kind="ExternalInput")
    b1_d = nc.dram_tensor("b1t", [128, 32], F32, kind="ExternalInput")
    b2_d = nc.dram_tensor("b2t", [128, 8], F32, kind="ExternalInput")
    cq_d = nc.dram_tensor("cq", [HALF, T], F32, kind="ExternalInput")
    sq_d = nc.dram_tensor("sq", [HALF, T], F32, kind="ExternalInput")
    ck_d = nc.dram_tensor("ck", [HALF, T], F32, kind="ExternalInput")
    sk_d = nc.dram_tensor("sk", [HALF, T], F32, kind="ExternalInput")
    ckw_d = nc.dram_tensor("ckw", [WRM, HALF], F32, kind="ExternalInput")
    skw_d = nc.dram_tensor("skw", [WRM, HALF], F32, kind="ExternalInput")
    tri_d = nc.dram_tensor("tri", [128, 128], BF16, kind="ExternalInput")
    idb_d = nc.dram_tensor("idb", [128, 128], BF16, kind="ExternalInput")
    idf_d = nc.dram_tensor("idf", [128, 128], F32, kind="ExternalInput")
    out_d = nc.dram_tensor("out", [T, D], F32, kind="ExternalOutput")

    dbg = "ExternalOutput" if DEBUG else "Internal"
    r1_d = nc.dram_tensor("r1_spill", [T, D], F32, kind=dbg)
    h_d = nc.dram_tensor("h_spill", [4, FF, 512], F32, kind=dbg)
    pm_d = nc.dram_tensor("pm_bounce", [128], F32)
    cc1_in = nc.dram_tensor("cc1_in", [1], F32)
    cc1_out = nc.dram_tensor("cc1_out", [1], F32)
    cc2_in = nc.dram_tensor("cc2_in", [1], F32)
    cc2_out = nc.dram_tensor("cc2_out", [1], F32)
    if DEBUG:
        kx_dbg = nc.dram_tensor("kx_dbg", [128, 8 * T], BF16, kind="ExternalOutput")
        vt_dbg = nc.dram_tensor("vt_dbg", [128, NT * D], BF16, kind="ExternalOutput")
        sin_dbg = nc.dram_tensor("sin_dbg", [128, 8 * D], BF16, kind="ExternalOutput")
        st_dbg = nc.dram_tensor("st_dbg", [128, 64], F32, kind="ExternalOutput")
        hqq_dbg = nc.dram_tensor("hqq_dbg", [4, 128, 32 * HALF], BF16,
                                 kind="ExternalOutput")
        yf_dbg = nc.dram_tensor("yf_dbg", [4, 128, 8 * HALF], F32,
                                kind="ExternalOutput")

    with ExitStack() as top:
        SB = lambda n, s, d: top.enter_context(nc.sbuf_tensor(n, s, d))
        sem = lambda n: top.enter_context(nc.semaphore(n))

        tri = SB("tri_sb", [128, 128], BF16)
        idb = SB("idb_sb", [128, 128], BF16)
        idf = SB("idf_sb", [128, 128], F32)
        b1t = SB("b1t_sb", [128, 32], F32)
        b2t = SB("b2t_sb", [128, 8], F32)
        stat = SB("stat_sb", [128, 16], F32)
        stat2 = SB("stat2_sb", [128, 16], F32)
        nmean = SB("nmean_sb", [128, 16], F32)
        rstd = SB("rstd_sb", [128, 16], F32)
        xmaxr = SB("xmaxr_sb", [128, 16], F32)
        statb = SB("statb_sb", [128, 16], F32)
        stat2b = SB("stat2b_sb", [128, 16], F32)
        nmeanb = SB("nmeanb_sb", [128, 16], F32)
        rstdb = SB("rstdb_sb", [128, 16], F32)
        sq_scr = SB("sq_scr", [128, D], F32)
        redrow = SB("redrow", [1, 128], F32)
        redsc = SB("redsc", [1, 1], F32)
        gmax1 = SB("gmax1", [128, 1], F32)
        sinsc = SB("sinsc", [128, 1], F32)
        rinv1 = SB("rinv1", [128, 1], F32)
        gmax2 = SB("gmax2", [128, 1], F32)
        shsc = SB("shsc", [128, 1], F32)
        hrec = SB("hrec", [128, 1], F32)
        s1v = SB("s1v", [128, 32], F32)
        qv2 = SB("qv2", [128, 32], F32)
        s2v = SB("s2v", [128, 8], F32)
        b1i = SB("b1i", [128, 32], F32)
        b2i = SB("b2i", [128, 8], F32)
        am1 = SB("am1", [128, 32], F32)
        am2 = SB("am2", [128, 8], F32)
        rc1 = SB("rc1", [128, 32], F32)
        rc2 = SB("rc2", [128, 8], F32)
        hrm4 = SB("hrm4", [128, 128], F32)
        hrm = SB("hrm", [128, 32], F32)
        hsc = SB("hsc", [128, 32], F32)
        lnsc = SB("lnsc", [128, 16], F32)
        xm1 = SB("xm1", [128, 1], F32)
        hm1 = SB("hm1", [128, 1], F32)
        ivec = SB("ivec", [128, 32], I32)

        with ExitStack() as scopeA:
            SA = lambda n, s, d: scopeA.enter_context(nc.sbuf_tensor(n, s, d))
            wqkv = SA("wqkv", [128, 3 * 8 * D], BF16)
            sin_sb = SA("sin_sb", [128, 8 * D], BF16)

            w_ld = sem("w_ld"); w_lg = sem("w_lg"); w_pet = sem("w_pet")
            w_evt = sem("w_evt"); w_pek = sem("w_pek"); w_evk = sem("w_evk")
            w_vx = sem("w_vx"); w_pes = sem("w_pes"); w_evs = sem("w_evs")

            with ExitStack() as wctx:
                WSB = lambda n, s, d: wctx.enter_context(nc.sbuf_tensor(n, s, d))
                xwbf = WSB("xwbf", [128, NW * D], BF16)
                xwfm = WSB("xwfm", [128, 8 * WRM], BF16)
                kxw = WSB("kxw", [128, NW * D], BF16)
                vw = WSB("vw", [128, NW * D], BF16)
                kwraw = WSB("kwraw", [128, 2 * D], BF16)
                twbl = WSB("twbl", [128, 2 * D], F32)
                xs1 = WSB("xs1", [128, HALF], F32)
                xs2 = WSB("xs2", [128, HALF], F32)

                with ExitStack() as pctx:
                    ps_bf = pctx.enter_context(
                        nc.psum_tensor("ps_bf", [128, 2 * 1024], BF16))
                    ps_f = pctx.enter_context(
                        nc.psum_tensor("ps_f", [128, 4 * HALF], F32))

                    with nc.Block() as blk:
                        @blk.sync
                        def _(s):
                            for t_, d_ in ((tri, tri_d), (idb, idb_d), (idf, idf_d),
                                           (b1t, b1_d), (b2t, b2_d)):
                                s.dma_start(out=t_[:], in_=d_[:]).then_inc(w_ld, 16)
                            for m in range(NW):
                                p = m % 2
                                if m >= 2:
                                    s.wait_ge(w_vx, m - 1)
                                s.dma_start(out=twbl[:, p * D:p * D + HALF],
                                            in_=ckw_d[m * 128:(m + 1) * 128, :]
                                            ).then_inc(w_ld, 16)
                                s.dma_start(out=twbl[:, p * D + HALF:(p + 1) * D],
                                            in_=skw_d[m * 128:(m + 1) * 128, :]
                                            ).then_inc(w_ld, 16)

                        @blk.gpsimd
                        def _(g):
                            dma3(g, xwbf[:], x_warm.ap(), NW, D).then_inc(w_lg, 16)
                            for i, wd in enumerate((wq_d, wk_d, wv_d)):
                                dma3(g, wqkv[:, i * 8 * D:(i + 1) * 8 * D],
                                     wd.ap(), 8, D).then_inc(w_lg, 16)

                        @blk.tensor
                        def _(t):
                            t.wait_ge(w_lg, 64)
                            t.wait_ge(w_ld, 144)
                            for d in range(8):
                                p = d % 2
                                if d >= 2:
                                    t.wait_ge(w_evt, d - 1)
                                for wt in range(NW):
                                    ins = t.transpose(
                                        ps_bf[:, p * 1024 + wt * 128:
                                              p * 1024 + (wt + 1) * 128],
                                        xwbf[:, wt * D + d * 128:wt * D + (d + 1) * 128],
                                        idb[:])
                                ins.then_inc(w_pet, 1)
                            t.wait_ge(w_evt, 8)
                            i = 0
                            for m in range(NW):
                                for which in (1, 2):
                                    for hv in range(2):
                                        slot = i % 4
                                        if i >= 4:
                                            t.wait_ge(w_evk, i - 3)
                                        for d in range(8):
                                            ins = t.matmul(
                                                ps_f[:, slot * HALF:(slot + 1) * HALF],
                                                xwfm[:, d * WRM + m * 128:
                                                     d * WRM + (m + 1) * 128],
                                                wqkv[:, which * 8 * D + d * D + hv * HALF:
                                                     which * 8 * D + d * D + hv * HALF + HALF],
                                                start=(d == 0), stop=(d == 7))
                                        ins.then_inc(w_pek, 1)
                                        i += 1

                        @blk.scalar
                        def _(s):
                            for d in range(8):
                                p = d % 2
                                s.wait_ge(w_pet, d + 1)
                                s.copy(out=xwfm[:, d * WRM:(d + 1) * WRM],
                                       in_=ps_bf[:, p * 1024:p * 1024 + WRM]
                                       ).then_inc(w_evt, 1)
                            i = 0
                            for m in range(NW):
                                for which in (1, 2):
                                    for hv in range(2):
                                        slot = i % 4
                                        s.wait_ge(w_pek, i + 1)
                                        if which == 1:
                                            if m >= 2:
                                                s.wait_ge(w_vx, m - 1)
                                            dst = kwraw[:, (m % 2) * D + hv * HALF:
                                                        (m % 2) * D + hv * HALF + HALF]
                                        else:
                                            dst = vw[:, m * D + hv * HALF:
                                                     m * D + hv * HALF + HALF]
                                        s.copy(out=dst,
                                               in_=ps_f[:, slot * HALF:(slot + 1) * HALF]
                                               ).then_inc(w_evk, 1)
                                        i += 1

                        @blk.vector
                        def _(v):
                            for m in range(NW):
                                p = m % 2
                                v.wait_ge(w_evk, 4 * m + 2)
                                v.wait_ge(w_ld, 80 + 32 * min(NW, m + 2))
                                cw = twbl[:, p * D:p * D + HALF]
                                sw = twbl[:, p * D + HALF:(p + 1) * D]
                                k1 = kwraw[:, (m % 2) * D:(m % 2) * D + HALF]
                                k2 = kwraw[:, (m % 2) * D + HALF:(m % 2) * D + D]
                                v.tensor_tensor(xs1[:], k1, cw, op=ALU.mult)
                                v.tensor_tensor(xs2[:], k2, sw, op=ALU.mult)
                                v.tensor_tensor(kxw[:, m * D:m * D + HALF],
                                                xs1[:], xs2[:], op=ALU.subtract)
                                v.tensor_tensor(xs1[:], k2, cw, op=ALU.mult)
                                v.tensor_tensor(xs2[:], k1, sw, op=ALU.mult)
                                v.tensor_tensor(kxw[:, m * D + HALF:(m + 1) * D],
                                                xs1[:], xs2[:], op=ALU.add
                                                ).then_inc(w_vx, 1)

                with ExitStack() as pctx:
                    ps_si = pctx.enter_context(
                        nc.psum_tensor("ps_si", [128, 8 * HALF], F32))

                    with nc.Block() as blk:
                        @blk.tensor
                        def _(t):
                            for pas in range(2):
                                if pas == 1:
                                    t.wait_ge(w_evs, 8)
                                for m in range(NW):
                                    for dk in range(4):
                                        for hv in range(2):
                                            ins = t.matmul(
                                                ps_si[:, (dk * 2 + hv) * HALF:
                                                      (dk * 2 + hv + 1) * HALF],
                                                kxw[:, m * D + (pas * 4 + dk) * 128:
                                                    m * D + (pas * 4 + dk) * 128 + 128],
                                                vw[:, m * D + hv * HALF:
                                                   m * D + hv * HALF + HALF],
                                                start=(m == 0), stop=(m == NW - 1))
                                ins.then_inc(w_pes, 1)

                        @blk.scalar
                        def _(s):
                            for pas in range(2):
                                s.wait_ge(w_pes, pas + 1)
                                for dk in range(4):
                                    for hv in range(2):
                                        s.copy(out=sin_sb[:, (pas * 4 + dk) * D + hv * HALF:
                                                          (pas * 4 + dk) * D + hv * HALF + HALF],
                                               in_=ps_si[:, (dk * 2 + hv) * HALF:
                                                         (dk * 2 + hv + 1) * HALF]
                                               ).then_inc(w_evs, 1)

            a_lg = sem("a_lg"); a_tx = sem("a_tx"); a_pet = sem("a_pet")
            a_evt = sem("a_evt"); a_tbl = sem("a_tbl"); a_vt = sem("a_vt")
            a_pea = sem("a_pea"); a_eva = sem("a_eva"); a_vxa = sem("a_vxa")
            a_pes = sem("a_pes"); a_evs = sem("a_evs"); a_peo = sem("a_peo")
            a_pfo = sem("a_pfo"); a_ldr = sem("a_ldr"); a_str = sem("a_str")
            a_sq = sem("a_sq")
            PEA_G = 24

            if not _on("attn"):
                return nc
            with ExitStack() as actx:
                ASB = lambda n, s, d: actx.enter_context(nc.sbuf_tensor(n, s, d))
                kx = ASB("kx", [128, 8 * T], BF16)
                vt = ASB("vt", [128, NT * D], BF16)
                xbf = ASB("xbf", [128, 4 * D], BF16)
                xfm = ASB("xfm", [128, 8 * HALF], BF16)
                qx = ASB("qx", [128, 8 * HALF], BF16)
                raw = ASB("raw", [128, 8 * HALF], BF16)
                tblc = ASB("tblc", [128, 4 * HALF], F32)
                tbls = ASB("tbls", [128, 4 * HALF], F32)
                scb = ASB("scb", [128, 2 * 128], BF16)
                xr = ASB("xr", [128, D], F32)
                r1b = ASB("r1b", [128, 2 * D], F32)
                ax1 = ASB("ax1", [128, HALF], F32)
                ax2 = ASB("ax2", [128, HALF], F32)
                ps_main = actx.enter_context(
                    nc.psum_tensor("ps_main", [128, 4 * HALF], F32))
                ps_sc = actx.enter_context(
                    nc.psum_tensor("ps_sc", [128, 2 * HALF], F32))
                ps_bf2 = actx.enter_context(
                    nc.psum_tensor("ps_bf2", [128, 2 * 1024], BF16))

                with nc.Block() as blk:
                    @blk.gpsimd
                    def _(g_):
                        for g in range(NG):
                            if g >= 1:
                                g_.wait_ge(a_tx, g)
                            sl = 0
                            src = x_own.ap()[g * 512:(g + 1) * 512, :]
                            dma3(g_, xbf[:, sl:sl + 4 * D], src, 4, D).then_inc(a_lg, 16)

                    @blk.sync
                    def _(s):
                        for g in range(NG):
                            if g >= 1:
                                s.wait_ge(a_vxa, g * 8)
                            for tb, dram in ((tblc, ck_d), (tbls, sk_d)):
                                dma3(s, tb[:], dram.ap()[:, g * HALF:(g + 1) * HALF],
                                     4, HALF).then_inc(a_tbl, 16)
                            s.wait_ge(a_vxa, g * 8 + 4)
                            for tb, dram in ((tblc, cq_d), (tbls, sq_d)):
                                dma3(s, tb[:], dram.ap()[:, g * HALF:(g + 1) * HALF],
                                     4, HALF).then_inc(a_tbl, 16)
                            for nn in range(4):
                                n = g * 4 + nn
                                if n >= 1:
                                    s.wait_ge(a_vt, n)
                                s.dma_start(out=xr[:],
                                            in_=x_own.ap()[n * 128:(n + 1) * 128, :]
                                            ).then_inc(a_ldr, 16)
                                s.wait_ge(a_vt, n + 1)
                                s.dma_start(out=r1_d.ap()[n * 128:(n + 1) * 128, :],
                                            in_=r1b[:, (n % 2) * D:(n % 2 + 1) * D]
                                            ).then_inc(a_str, 16)
                        s.wait_ge(a_str, 16 * NT)

                    @blk.tensor
                    def _(t):
                        for g in range(NG):
                            t.wait_ge(a_lg, (g + 1) * 16)
                            for d in range(8):
                                p = d % 2
                                if g * 8 + d >= 2:
                                    t.wait_ge(a_evt, g * 8 + d - 1)
                                xsl = 0
                                for wt in range(4):
                                    ins = t.transpose(
                                        ps_bf2[:, p * 1024 + wt * 128:
                                               p * 1024 + (wt + 1) * 128],
                                        xbf[:, xsl + wt * D + d * 128:
                                            xsl + wt * D + (d + 1) * 128],
                                        idb[:])
                                ins.then_inc(a_pet, 1)
                            t.nop().then_inc(a_tx, 1)
                            t.wait_ge(a_evt, (g + 1) * 8)
                            t.wait_ge(a_pfo, g * 4)
                            i0 = g * PEA_G
                            for fc in range(8):
                                i = i0 + fc
                                if i >= 4:
                                    t.wait_ge(a_eva, i - 3)
                                for d in range(8):
                                    ins = t.matmul(
                                        ps_main[:, (i % 4) * HALF:(i % 4 + 1) * HALF],
                                        wqkv[:, 8 * D + d * D + fc * 128:
                                             8 * D + d * D + fc * 128 + 128],
                                        xfm[:, d * HALF:(d + 1) * HALF],
                                        start=(d == 0), stop=(d == 7))
                                ins.then_inc(a_pea, 1)
                            for fc in range(8):
                                i = i0 + 8 + fc
                                t.wait_ge(a_eva, i - 3)
                                for d in range(8):
                                    ins = t.matmul(
                                        ps_main[:, (i % 4) * HALF:(i % 4 + 1) * HALF],
                                        wqkv[:, d * D + fc * 128:d * D + fc * 128 + 128],
                                        xfm[:, d * HALF:(d + 1) * HALF],
                                        start=(d == 0), stop=(d == 7))
                                ins.then_inc(a_pea, 1)
                            for tb in range(4):
                                for hv in range(2):
                                    i = i0 + 16 + tb * 2 + hv
                                    t.wait_ge(a_eva, i - 3)
                                    for d in range(8):
                                        ins = t.matmul(
                                            ps_main[:, (i % 4) * HALF:(i % 4 + 1) * HALF],
                                            xfm[:, d * HALF + tb * 128:
                                                d * HALF + (tb + 1) * 128],
                                            wqkv[:, 16 * D + d * D + hv * HALF:
                                                 16 * D + d * D + hv * HALF + HALF],
                                            start=(d == 0), stop=(d == 7))
                                    ins.then_inc(a_pea, 1)
                            t.wait_ge(a_vxa, g * 8 + 8)
                            t.wait_ge(a_eva, (g + 1) * PEA_G)
                            for nn in range(4):
                                n = g * 4 + nn
                                po = (n % 2) * D
                                if n >= 2:
                                    t.wait_ge(a_pfo, n - 1)
                                for d in range(8):
                                    for hv in range(2):
                                        t.matmul(
                                            ps_main[:, po + hv * HALF:po + hv * HALF + HALF],
                                            qx[:, d * HALF + nn * 128:
                                               d * HALF + nn * 128 + 128],
                                            sin_sb[:, d * D + hv * HALF:
                                                   d * D + hv * HALF + HALF],
                                            start=(d == 0), stop=False)
                                base_s = pes_before(n)
                                for m in range(n + 1):
                                    sl_s = m % 2
                                    if m >= 2:
                                        t.wait_ge(a_evs, base_s + m - 1)
                                    for d in range(8):
                                        ins = t.matmul(
                                            ps_sc[:, sl_s * HALF:sl_s * HALF + 128],
                                            kx[:, d * T + m * 128:d * T + m * 128 + 128],
                                            qx[:, d * HALF + nn * 128:
                                               d * HALF + nn * 128 + 128],
                                            start=(d == 0), stop=(d == 7))
                                    ins.then_inc(a_pes, 1)
                                    if m > 0:
                                        t.wait_ge(a_evs, base_s + m)
                                        for hv in range(2):
                                            t.matmul(
                                                ps_main[:, po + hv * HALF:
                                                        po + hv * HALF + HALF],
                                                scb[:, ((m - 1) % 2) * 128:
                                                    ((m - 1) % 2 + 1) * 128],
                                                vt[:, (m - 1) * D + hv * HALF:
                                                   (m - 1) * D + hv * HALF + HALF],
                                                start=False, stop=False)
                                t.wait_ge(a_evs, base_s + n + 1)
                                for hv in range(2):
                                    ins = t.matmul(
                                        ps_main[:, po + hv * HALF:po + hv * HALF + HALF],
                                        scb[:, (n % 2) * 128:(n % 2 + 1) * 128],
                                        vt[:, n * D + hv * HALF:n * D + hv * HALF + HALF],
                                        start=False, stop=True)
                                ins.then_inc(a_peo, 1)

                    @blk.scalar
                    def _(s):
                        for g in range(NG):
                            for d in range(8):
                                p = d % 2
                                s.wait_ge(a_pet, g * 8 + d + 1)
                                s.copy(out=xfm[:, d * HALF:(d + 1) * HALF],
                                       in_=ps_bf2[:, p * 1024:p * 1024 + HALF]
                                       ).then_inc(a_evt, 1)
                            i0 = g * PEA_G
                            for j in range(PEA_G):
                                i = i0 + j
                                s.wait_ge(a_pea, i + 1)
                                slot = i % 4
                                if j < 8:
                                    if j == 0:
                                        s.wait_ge(a_vxa, g * 8)
                                    dst = raw[:, j * HALF:(j + 1) * HALF]
                                elif j < 16:
                                    if j == 8:
                                        s.wait_ge(a_vxa, g * 8 + 4)
                                    dst = raw[:, (j - 8) * HALF:(j - 7) * HALF]
                                else:
                                    tb, hv = divmod(j - 16, 2)
                                    tg = g * 4 + tb
                                    dst = vt[:, tg * D + hv * HALF:tg * D + hv * HALF + HALF]
                                s.copy(out=dst,
                                       in_=ps_main[:, slot * HALF:(slot + 1) * HALF]
                                       ).then_inc(a_eva, 1)
                            for nn in range(4):
                                n = g * 4 + nn
                                s.wait_ge(a_vt, n + 1)
                                s.activation(out=sq_scr[:],
                                             in_=r1b[:, (n % 2) * D:(n % 2 + 1) * D],
                                             func=AF.Square,
                                             accum_out=stat2[:, n:n + 1]).then_inc(a_sq, 1)

                    @blk.vector
                    def _(v):
                        v.wait_ge(w_ld, 80)
                        for g in range(NG):
                            for which in range(2):
                                for j in range(4):
                                    v.wait_ge(a_eva, g * PEA_G + which * 8 + j + 5)
                                    v.wait_ge(a_tbl, g * 64 + 32 + which * 32)
                                    cw = tblc[:, j * HALF:(j + 1) * HALF]
                                    sw = tbls[:, j * HALF:(j + 1) * HALF]
                                    r1_ = raw[:, j * HALF:(j + 1) * HALF]
                                    r2_ = raw[:, (j + 4) * HALF:(j + 5) * HALF]
                                    if which == 0:
                                        o1 = kx[:, j * T + g * HALF:j * T + (g + 1) * HALF]
                                        o2 = kx[:, (j + 4) * T + g * HALF:
                                                (j + 4) * T + (g + 1) * HALF]
                                    else:
                                        o1 = qx[:, j * HALF:(j + 1) * HALF]
                                        o2 = qx[:, (j + 4) * HALF:(j + 5) * HALF]
                                    v.tensor_tensor(ax1[:], r1_, cw, op=ALU.mult)
                                    v.tensor_tensor(ax2[:], r2_, sw, op=ALU.mult)
                                    v.tensor_tensor(o1, ax1[:], ax2[:], op=ALU.subtract)
                                    v.tensor_tensor(ax1[:], r2_, cw, op=ALU.mult)
                                    v.tensor_tensor(ax2[:], r1_, sw, op=ALU.mult)
                                    v.tensor_tensor(o2, ax1[:], ax2[:], op=ALU.add
                                                    ).then_inc(a_vxa, 1)
                            for nn in range(4):
                                n = g * 4 + nn
                                base_s = pes_before(n)
                                for m in range(n + 1):
                                    v.wait_ge(a_pes, base_s + m + 1)
                                    sl_s = m % 2
                                    if m == n:
                                        v.tensor_tensor(
                                            scb[:, sl_s * 128:(sl_s + 1) * 128],
                                            ps_sc[:, sl_s * HALF:sl_s * HALF + 128],
                                            tri[:], op=ALU.mult).then_inc(a_evs, 1)
                                    else:
                                        v.tensor_copy(
                                            scb[:, sl_s * 128:(sl_s + 1) * 128],
                                            ps_sc[:, sl_s * HALF:sl_s * HALF + 128]
                                            ).then_inc(a_evs, 1)
                                po = (n % 2) * D
                                v.wait_ge(a_peo, n + 1)
                                v.wait_ge(a_ldr, (n + 1) * 16)
                                if n >= 2:
                                    v.wait_ge(a_str, n * 16)
                                    v.wait_ge(a_sq, n - 1)
                                v.tensor_tensor(r1b[:, (n % 2) * D:(n % 2 + 1) * D],
                                                ps_main[:, po:po + D], xr[:],
                                                op=ALU.add).then_inc(a_vt, 1)
                                v.nop().then_inc(a_pfo, 1)
                                r1_ = r1b[:, (n % 2) * D:(n % 2 + 1) * D]
                                v.reduce_sum(stat[:, n:n + 1], r1_, axis=AX.X)
                                v.reduce_max(xmaxr[:, n:n + 1], r1_, axis=AX.X)

        r_ld = sem("r_ld"); r_lg = sem("r_lg"); r_q = sem("r_q")
        r_pe = sem("r_pe"); r_ev = sem("r_ev"); r_vx = sem("r_vx")
        r_cc = sem("r_cc"); r_sq = sem("r_sq")
        wsl = [sem(f"wsl{k}") for k in range(4)]
        rsl = [sem(f"rsl{k}") for k in range(4)]
        x_q = sem("x_q"); x_pe = sem("x_pe"); x_ev = sem("x_ev")
        m_pe = sem("m_pe"); m_ev = sem("m_ev"); m_vr = sem("m_vr")
        msl = [sem(f"msl{k}") for k in range(4)]

        if not _on("f1"):
            return nc
        with ExitStack() as scopeF:
            SF = lambda n, s, d: scopeF.enter_context(nc.sbuf_tensor(n, s, d))
            w1qt = SF("w1qt", [128, 8 * FF], BF16)
            xqfm = SF("xqfm", [128, 8 * T], BF16)

            with ExitStack() as scopeF1:
                SF1 = lambda n, s, d: scopeF1.enter_context(nc.sbuf_tensor(n, s, d))
                w1c = SF1("w1c", [128, 4 * D], F32)
                w1i = SF1("w1i", [128, D], I32)
                w1b = SF1("w1b", [128, 2 * D], BF16)
                ps_w = scopeF1.enter_context(
                    nc.psum_tensor("ps_w", [128, 8 * D], BF16))

                with nc.Block() as blk:
                    @blk.sync
                    def _(s):
                        for fb in range(32):
                            if fb >= 4:
                                s.wait_ge(r_q, fb - 3)
                            s.dma_start(out=w1c[:, (fb % 4) * D:(fb % 4 + 1) * D],
                                        in_=w1_d.ap()[fb * 128:(fb + 1) * 128, :]
                                        ).then_inc(wsl[fb % 4], 16)

                    @blk.gpsimd
                    def _(g):
                        g.wait_ge(r_vx, 2)
                        g.dma_start(out=pm_d[:], in_=xm1[:, 0]).then_inc(r_lg, 16)
                        g.wait_ge(r_lg, 16)
                        g.dma_start(out=redrow[:], in_=pm_d[None, :]).then_inc(r_lg, 16)
                        g.wait_ge(r_vx, 3)
                        g.dma_start(out=cc1_in[:], in_=redsc[0, :]).then_inc(r_lg, 16)
                        g.wait_ge(r_lg, 48)
                        g.collective_compute(
                            "AllReduce", ALU.max,
                            replica_groups=[list(range(8))],
                            ins=[cc1_in.ap()], outs=[cc1_out.ap()],
                        ).then_inc(r_cc, 1)
                        g.wait_ge(r_cc, 1)
                        g.dma_start(out=gmax1[:],
                                    in_=cc1_out.ap()[None, :].broadcast_to((128, 1))
                                    ).then_inc(r_lg, 16)

                    @blk.vector
                    def _(v):
                        v.drain()
                        v.tensor_scalar(nmean[:], stat[:], -1.0 / D, None, op0=ALU.mult)
                        v.tensor_scalar(stat2[:], stat2[:], 1.0 / D, None, op0=ALU.mult)
                        v.drain()
                        v.tensor_tensor(lnsc[:], nmean[:], nmean[:], op=ALU.mult)
                        v.drain()
                        v.tensor_tensor(stat2[:], stat2[:], lnsc[:], op=ALU.subtract)
                        v.drain()
                        v.tensor_scalar(stat2[:], stat2[:], EPS_LN, None, op0=ALU.add)
                        v.drain()
                        v.reciprocal(stat2[:], stat2[:])
                        v.drain()
                        v.nop().then_inc(r_vx, 1)
                        v.wait_ge(r_sq, 1)
                        v.tensor_tensor(lnsc[:], xmaxr[:], nmean[:], op=ALU.add)
                        v.drain()
                        v.tensor_tensor(lnsc[:], lnsc[:], rstd[:], op=ALU.mult)
                        v.drain()
                        v.reduce_max(xm1[:], lnsc[:], axis=AX.X)
                        v.drain()
                        v.nop().then_inc(r_vx, 1)

                        def w1red(fb):
                            v.wait_ge(wsl[fb % 4], 16 * (fb // 4 + 1))
                            v.tensor_reduce(am1[:, fb:fb + 1],
                                            w1c[:, (fb % 4) * D:(fb % 4 + 1) * D],
                                            axis=AX.X, op=ALU.max,
                                            apply_absolute_value=True)

                        def w1q(fb):
                            v.tensor_scalar(am1[:, fb:fb + 1], am1[:, fb:fb + 1],
                                            EPS_Q, None, op0=ALU.max)
                            v.drain()
                            v.reciprocal(rc1[:, fb:fb + 1], am1[:, fb:fb + 1])
                            v.drain()
                            v.tensor_scalar(w1i[:], w1c[:, (fb % 4) * D:(fb % 4 + 1) * D],
                                            rc1[:, fb:fb + 1], 127.0,
                                            op0=ALU.mult, op1=ALU.mult)
                            if fb >= 2:
                                v.wait_ge(r_pe, fb - 1)
                            v.tensor_copy(w1b[:, (fb % 2) * D:(fb % 2 + 1) * D],
                                          w1i[:]).then_inc(r_q, 1)
                        w1red(0)
                        for fb in range(8):
                            w1red(fb + 1)
                            w1q(fb)
                        v.wait_ge(r_lg, 32)
                        v.reduce_max(redsc[:], redrow[:], axis=AX.X)
                        v.drain()
                        v.nop().then_inc(r_vx, 1)
                        for fb in range(8, 32):
                            if fb + 1 < 32:
                                w1red(fb + 1)
                            w1q(fb)
                        v.wait_ge(r_lg, 64)
                        v.tensor_scalar(gmax1[:], gmax1[:], EPS_Q, None, op0=ALU.max)
                        v.drain()
                        v.tensor_scalar(sinsc[:], gmax1[:], 1.0 / 255.0, None,
                                        op0=ALU.mult)
                        v.drain()
                        v.reciprocal(rinv1[:], sinsc[:])
                        v.tensor_scalar(s1v[:], am1[:], 1.0 / 127.0, None, op0=ALU.mult)
                        v.drain()
                        v.tensor_scalar(s1v[:], s1v[:], sinsc[:, 0:1], None, op0=ALU.mult)
                        v.drain()
                        v.reciprocal(lnsc[:], s1v[:, 0:16])
                        v.drain()
                        v.reciprocal(hrm4[:, 0:16], s1v[:, 16:32])
                        v.drain()
                        v.tensor_tensor(hsc[:, 0:16], b1t[:, 0:16], lnsc[:],
                                        op=ALU.mult)
                        v.tensor_tensor(hsc[:, 16:32], b1t[:, 16:32], hrm4[:, 0:16],
                                        op=ALU.mult)
                        v.drain()
                        v.tensor_copy(ivec[:], hsc[:])
                        v.drain()
                        v.tensor_copy(b1i[:], ivec[:])
                        v.drain()
                        v.nop().then_inc(r_vx, 1)

                    @blk.scalar
                    def _(s):
                        s.wait_ge(r_vx, 1)
                        s.sqrt(rstd[:], stat2[:]).then_inc(r_sq, 1)
                        for G in range(4):
                            s.wait_ge(r_pe, (G + 1) * 8)
                            for d in range(8):
                                s.copy(out=w1qt[:, d * FF + G * D:d * FF + (G + 1) * D],
                                       in_=ps_w[:, d * D:(d + 1) * D]).then_inc(r_ev, 1)

                    @blk.tensor
                    def _(t):
                        for fb in range(32):
                            G = fb // 8
                            t.wait_ge(r_q, fb + 1)
                            if G > 0 and fb % 8 == 0:
                                t.wait_ge(r_ev, G * 8)
                            for d in range(8):
                                ins = t.transpose(
                                    ps_w[:, d * D + (fb % 8) * 128:
                                         d * D + (fb % 8) * 128 + 128],
                                    w1b[:, (fb % 2) * D + d * 128:
                                        (fb % 2) * D + (d + 1) * 128],
                                    idb[:])
                            ins.then_inc(r_pe, 1)

            if not _on("f2a"):
                return nc
            with ExitStack() as scopeF2:
                SF2 = lambda n, s, d: scopeF2.enter_context(nc.sbuf_tensor(n, s, d))
                r1rl = SF2("r1rl", [128, 4 * D], F32)
                xqf = SF2("xqf", [128, 2 * D], F32)
                xqi = SF2("xqi", [128, 2 * D], I32)
                xqb = SF2("xqb", [128, 2 * D], BF16)

                with ExitStack() as pctx:
                    ps_w2 = pctx.enter_context(
                        nc.psum_tensor("ps_w2", [128, 8 * D], BF16))

                    with nc.Block() as blk:
                        @blk.sync
                        def _(s):
                            for t_ in range(NT):
                                if t_ < 4:
                                    s.wait_ge(r_q, 29 + t_)
                                else:
                                    s.wait_ge(x_q, t_ - 3)
                                s.dma_start(out=r1rl[:, (t_ % 4) * D:(t_ % 4 + 1) * D],
                                            in_=r1_d.ap()[t_ * 128:(t_ + 1) * 128, :]
                                            ).then_inc(rsl[t_ % 4], 16)

                        @blk.vector
                        def _(v):
                            for t_ in range(NT):
                                p = (t_ % 2) * D
                                v.wait_ge(rsl[t_ % 4], 16 * (t_ // 4 + 1))
                                src = r1rl[:, (t_ % 4) * D:(t_ % 4 + 1) * D]
                                v.tensor_scalar(xqf[:, p:p + D], src, nmean[:, t_:t_ + 1],
                                                rstd[:, t_:t_ + 1],
                                                op0=ALU.add, op1=ALU.mult)
                                v.tensor_scalar(xqf[:, p:p + D], xqf[:, p:p + D],
                                                rinv1[:, 0:1], 0.0,
                                                op0=ALU.mult, op1=ALU.max)
                                v.tensor_copy(xqi[:, p:p + D], xqf[:, p:p + D])
                                if t_ >= 2:
                                    v.wait_ge(x_pe, t_ - 1)
                                v.tensor_copy(xqb[:, p:p + D],
                                              xqi[:, p:p + D]).then_inc(x_q, 1)

                        @blk.tensor
                        def _(t):
                            for t_ in range(NT):
                                G = t_ // 8
                                t.wait_ge(x_q, t_ + 1)
                                if G > 0 and t_ % 8 == 0:
                                    t.wait_ge(x_ev, G * 8)
                                for d in range(8):
                                    ins = t.transpose(
                                        ps_w2[:, d * D + (t_ % 8) * 128:
                                              d * D + (t_ % 8) * 128 + 128],
                                        xqb[:, (t_ % 2) * D + d * 128:
                                            (t_ % 2) * D + (d + 1) * 128],
                                        idb[:])
                                ins.then_inc(x_pe, 1)

                        @blk.scalar
                        def _(s):
                            for G in range(2):
                                s.wait_ge(x_pe, (G + 1) * 8)
                                for d in range(8):
                                    s.copy(out=xqfm[:, d * T + G * D:d * T + (G + 1) * D],
                                           in_=ps_w2[:, d * D:(d + 1) * D]
                                           ).then_inc(x_ev, 1)

                if not _on("f2b"):
                    return nc
                with ExitStack() as pctx:
                    ps_mm1 = pctx.enter_context(
                        nc.psum_tensor("ps_mm1", [128, 4 * HALF], F32))
                    htb = pctx.enter_context(
                        nc.sbuf_tensor("htb", [128, 4 * HALF], F32))

                    with nc.Block() as blk:
                        @blk.tensor
                        def _(t):
                            for tt in range(4):
                                for fb in range(32):
                                    i = tt * 32 + fb
                                    if fb == 0:
                                        t.wait_ge(x_ev, 8 if tt < 2 else 16)
                                    if i >= 4:
                                        t.wait_ge(m_ev, i - 3)
                                    for d in range(8):
                                        ins = t.matmul(
                                            ps_mm1[:, (i % 4) * HALF:(i % 4 + 1) * HALF],
                                            w1qt[:, d * FF + fb * 128:
                                                 d * FF + fb * 128 + 128],
                                            xqfm[:, d * T + tt * 512:d * T + (tt + 1) * 512],
                                            start=(d == 0), stop=(d == 7))
                                    ins.then_inc(m_pe, 1)

                        @blk.scalar
                        def _(s):
                            for i in range(128):
                                fb = i % 32
                                s.wait_ge(m_pe, i + 1)
                                if i >= 4:
                                    s.wait_ge(m_vr, i - 3)
                                    s.wait_ge(msl[i % 4], 16 * (i // 4))
                                s.activation(out=htb[:, (i % 4) * HALF:(i % 4 + 1) * HALF],
                                             in_=ps_mm1[:, (i % 4) * HALF:(i % 4 + 1) * HALF],
                                             func=AF.Relu, bias=b1i[:, fb:fb + 1],
                                             scale=1.0).then_inc(m_ev, 1)

                        @blk.vector
                        def _(v):
                            for i in range(128):
                                tt, fb = divmod(i, 32)
                                v.wait_ge(m_ev, i + 1)
                                v.reduce_max(hrm4[:, fb * 4 + tt:fb * 4 + tt + 1],
                                             htb[:, (i % 4) * HALF:(i % 4 + 1) * HALF],
                                             axis=AX.X).then_inc(m_vr, 1)

                        @blk.gpsimd
                        def _(g):
                            for i in range(128):
                                tt, fb = divmod(i, 32)
                                g.wait_ge(m_ev, i + 1)
                                g.dma_start(
                                    out=h_d.ap()[tt, fb * 128:(fb + 1) * 128, :],
                                    in_=htb[:, (i % 4) * HALF:(i % 4 + 1) * HALF]
                                ).then_inc(msl[i % 4], 16)
                            for k in range(4):
                                g.wait_ge(msl[k], 16 * 32)

        h_vx = sem("h_vx"); h_lg = sem("h_lg"); h_cc = sem("h_cc")
        v_q = sem("v_q"); tp2 = sem("tp2"); te2 = sem("te2")
        w2l = [sem(f"w2l{k}") for k in range(4)]
        q_ld = sem("q_ld"); q_v = sem("q_v"); y_pe = sem("y_pe")
        y_ev = sem("y_ev"); yt_pe = sem("yt_pe"); yt_fr = sem("yt_fr")
        o_ld = sem("o_ld"); o_v = sem("o_v"); o_sq2 = sem("o_sq2")
        o_rec = sem("o_rec"); o_rsq = sem("o_rsq"); o_vv = sem("o_vv")
        o_st = sem("o_st"); dbg_q = sem("dbg_q"); dbg_y = sem("dbg_y")
        qh = [sem(f"qh{k}") for k in range(4)]
        osl = [sem(f"osl{k}") for k in range(4)]

        if not _on("g1"):
            return nc
        with ExitStack() as scopeG:
            SG = lambda n, s, d: scopeG.enter_context(nc.sbuf_tensor(n, s, d))
            w2qt = SG("w2qt", [128, 32 * D], BF16)
            hstg = SG("hstg", [128, 3 * HALF], F32)
            hqq = SG("hqq", [128, 2 * 32 * HALF], BF16)

            with ExitStack() as scopeG1:
                SG1 = lambda n, s, d: scopeG1.enter_context(nc.sbuf_tensor(n, s, d))
                w2c = SG1("w2c", [128, 2 * FF], F32)
                w2i = SG1("w2i", [128, FF], I32)
                w2b = SG1("w2b", [128, 2 * FF], BF16)
                ps_w3 = scopeG1.enter_context(
                    nc.psum_tensor("ps_w3", [128, 2 * 8 * 128], BF16))

                with nc.Block() as blk:
                    @blk.sync
                    def _(s):
                        for dc in range(8):
                            if dc == 0:
                                s.wait_ge(m_pe, 128)
                            elif dc >= 2:
                                s.wait_ge(v_q, dc - 1)
                            s.dma_start(out=w2c[:, (dc % 2) * FF:(dc % 2 + 1) * FF],
                                        in_=w2_d.ap()[dc * 128:(dc + 1) * 128, :]
                                        ).then_inc(w2l[dc % 2], 16)

                    @blk.vector
                    def _(v):
                        v.wait_ge(m_vr, 128)
                        v.drain()
                        for fb in range(32):
                            v.reduce_max(hrm[:, fb:fb + 1],
                                         hrm4[:, fb * 4:(fb + 1) * 4], axis=AX.X)
                        v.drain()
                        v.tensor_tensor(hsc[:], hrm[:], s1v[:], op=ALU.mult)
                        v.drain()
                        v.reduce_max(hm1[:], hsc[:], axis=AX.X)
                        v.drain()
                        v.nop().then_inc(h_vx, 1)
                        v.wait_ge(h_lg, 32)
                        v.reduce_max(redsc[:], redrow[:], axis=AX.X)
                        v.drain()
                        v.nop().then_inc(h_vx, 1)

                        def w2red(dc):
                            v.wait_ge(w2l[dc % 2], 16 * (dc // 2 + 1))
                            v.tensor_reduce(am2[:, dc:dc + 1],
                                            w2c[:, (dc % 2) * FF:(dc % 2 + 1) * FF],
                                            axis=AX.X, op=ALU.max,
                                            apply_absolute_value=True)
                        w2red(0)
                        for dc in range(8):
                            if dc + 1 < 8:
                                w2red(dc + 1)
                            v.tensor_scalar(am2[:, dc:dc + 1], am2[:, dc:dc + 1],
                                            EPS_Q, None, op0=ALU.max)
                            v.drain()
                            v.reciprocal(rc2[:, dc:dc + 1], am2[:, dc:dc + 1])
                            v.drain()
                            v.tensor_scalar(w2i[:], w2c[:, (dc % 2) * FF:(dc % 2 + 1) * FF],
                                            rc2[:, dc:dc + 1], 127.0,
                                            op0=ALU.mult, op1=ALU.mult)
                            if dc >= 2:
                                v.wait_ge(tp2, (dc - 1) * 4)
                            v.tensor_copy(w2b[:, (dc % 2) * FF:(dc % 2 + 1) * FF],
                                          w2i[:]).then_inc(v_q, 1)
                        v.wait_ge(h_lg, 64)
                        v.tensor_scalar(gmax2[:], gmax2[:], EPS_Q, None, op0=ALU.max)
                        v.drain()
                        v.tensor_scalar(shsc[:], gmax2[:], 1.0 / 255.0, None,
                                        op0=ALU.mult)
                        v.drain()
                        v.reciprocal(hrec[:], shsc[:])
                        v.drain()
                        v.tensor_scalar(qv2[:], s1v[:], hrec[:, 0:1], None, op0=ALU.mult)
                        v.tensor_scalar(s2v[:], am2[:], 1.0 / 127.0, None, op0=ALU.mult)
                        v.drain()
                        v.tensor_scalar(s2v[:], s2v[:], shsc[:, 0:1], None, op0=ALU.mult)
                        v.drain()
                        v.reciprocal(lnsc[:, 0:8], s2v[:])
                        v.drain()
                        v.tensor_tensor(hsc[:, 0:8], b2t[:], lnsc[:, 0:8],
                                        op=ALU.mult)
                        v.drain()
                        v.tensor_copy(ivec[:, 0:8], hsc[:, 0:8])
                        v.drain()
                        v.tensor_copy(b2i[:], ivec[:, 0:8])
                        v.drain()
                        v.nop().then_inc(h_vx, 1)

                    @blk.gpsimd
                    def _(g):
                        g.wait_ge(h_vx, 1)
                        g.dma_start(out=pm_d[:], in_=hm1[:, 0]).then_inc(h_lg, 16)
                        g.wait_ge(h_lg, 16)
                        g.dma_start(out=redrow[:], in_=pm_d[None, :]).then_inc(h_lg, 16)
                        g.wait_ge(h_vx, 2)
                        g.dma_start(out=cc2_in[:], in_=redsc[0, :]).then_inc(h_lg, 16)
                        g.wait_ge(h_lg, 48)
                        g.collective_compute(
                            "AllReduce", ALU.max,
                            replica_groups=[list(range(8))],
                            ins=[cc2_in.ap()], outs=[cc2_out.ap()],
                        ).then_inc(h_cc, 1)
                        g.wait_ge(h_cc, 1)
                        g.dma_start(out=gmax2[:],
                                    in_=cc2_out.ap()[None, :].broadcast_to((128, 1))
                                    ).then_inc(h_lg, 16)

                    @blk.tensor
                    def _(t):
                        for dc in range(8):
                            t.wait_ge(v_q, dc + 1)
                            for g8 in range(4):
                                sg = dc * 4 + g8
                                if sg >= 2:
                                    t.wait_ge(te2, (sg - 1) * 8)
                                for fs_r in range(8):
                                    fs = g8 * 8 + fs_r
                                    ins = t.transpose(
                                        ps_w3[:, (sg % 2) * 1024 + fs_r * 128:
                                              (sg % 2) * 1024 + (fs_r + 1) * 128],
                                        w2b[:, (dc % 2) * FF + fs * 128:
                                            (dc % 2) * FF + fs * 128 + 128],
                                        idb[:])
                                ins.then_inc(tp2, 1)

                    @blk.scalar
                    def _(s):
                        for dc in range(8):
                            for g8 in range(4):
                                sg = dc * 4 + g8
                                s.wait_ge(tp2, sg + 1)
                                for fs_r in range(8):
                                    fs = g8 * 8 + fs_r
                                    s.copy(out=w2qt[:, fs * D + dc * 128:
                                                    fs * D + (dc + 1) * 128],
                                           in_=ps_w3[:, (sg % 2) * 1024 + fs_r * 128:
                                                     (sg % 2) * 1024 + (fs_r + 1) * 128]
                                           ).then_inc(te2, 1)

            if not _on("g2"):
                return nc
            with ExitStack() as scopeG2:
                SG2 = lambda n, s, d: scopeG2.enter_context(nc.sbuf_tensor(n, s, d))
                hqi = SG2("hqi", [128, 2 * HALF], I32)
                yf = SG2("yf", [128, 8 * HALF], F32)
                r1m = SG2("r1m", [128, 2 * D], F32)
                r1nm = SG2("r1nm", [128, D], F32)
                r2b = SG2("r2b", [128, 4 * D], F32)
                ps_y = scopeG2.enter_context(
                    nc.psum_tensor("ps_y", [128, 4 * HALF], F32))
                ps_yt = scopeG2.enter_context(
                    nc.psum_tensor("ps_yt", [128, 2 * D], F32))

                with nc.Block() as blk:
                    @blk.sync
                    def _(s):
                        for k in range(4):
                            s.wait_ge(msl[k], 16 * 32)
                        for idx in range(128):
                            tt, fb = divmod(idx, 32)
                            if idx >= 3:
                                s.wait_ge(q_v, idx - 2)
                            s.dma_start(
                                out=hstg[:, (idx % 3) * HALF:(idx % 3 + 1) * HALF],
                                in_=h_d.ap()[tt, fb * 128:(fb + 1) * 128, :]
                            ).then_inc(qh[idx % 3], 16)
                        s.wait_ge(o_st, 1)

                    @blk.gpsimd
                    def _(g):
                        for t_ in range(NT):
                            g.wait_ge(o_vv, t_ + 1)
                            g.dma_start(out=out_d.ap()[t_ * 128:(t_ + 1) * 128, :],
                                        in_=r2b[:, (t_ % 4) * D:(t_ % 4 + 1) * D]
                                        ).then_inc(osl[t_ % 4], 16)
                        for k in range(4):
                            g.wait_ge(osl[k], 16 * 4)
                        g.nop().then_inc(o_st, 1)

                    @blk.vector
                    def _(v):
                        def quant(tt):
                            if tt >= 2:
                                v.wait_ge(y_pe, (tt - 1) * 8)
                                if DEBUG:
                                    v.wait_ge(dbg_q, 16 * (tt - 1))
                            hq0 = (tt % 2) * 32 * HALF
                            for fb in range(32):
                                idx = tt * 32 + fb
                                q = (fb % 2) * HALF
                                v.wait_ge(qh[idx % 3], 16 * (idx // 3 + 1))
                                v.tensor_scalar(hqi[:, q:q + HALF],
                                                hstg[:, (idx % 3) * HALF:(idx % 3 + 1) * HALF],
                                                qv2[:, fb:fb + 1], None, op0=ALU.mult)
                                v.tensor_copy(hqq[:, hq0 + fb * 512:hq0 + (fb + 1) * 512],
                                              hqi[:, q:q + HALF]).then_inc(q_v, 1)

                        def posttt(tt):
                            for db in range(8):
                                i = tt * 8 + db
                                v.wait_ge(y_pe, i + 1)
                                if DEBUG and tt >= 1 and db == 0:
                                    v.wait_ge(dbg_y, 16 * tt)
                                v.tensor_scalar(yf[:, db * HALF:(db + 1) * HALF],
                                                ps_y[:, (i % 4) * HALF:(i % 4 + 1) * HALF],
                                                b2i[:, db:db + 1], s2v[:, db:db + 1],
                                                op0=ALU.add, op1=ALU.mult).then_inc(y_ev, 1)
                            for ts4 in range(4):
                                t_ = tt * 4 + ts4
                                v.wait_ge(yt_pe, t_ + 1)
                                v.wait_ge(o_ld, 16 * min(NT, t_ + 2))
                                if t_ >= 4:
                                    v.wait_ge(osl[t_ % 4], 16 * (t_ // 4))
                                src = r1m[:, (t_ % 2) * D:(t_ % 2 + 1) * D]
                                v.tensor_scalar(r1nm[:], src, nmean[:, t_:t_ + 1],
                                                rstd[:, t_:t_ + 1],
                                                op0=ALU.add, op1=ALU.mult)
                                v.tensor_tensor(r2b[:, ts4 * D:(ts4 + 1) * D],
                                                ps_yt[:, (t_ % 2) * D:(t_ % 2 + 1) * D],
                                                r1nm[:], op=ALU.add)
                                v.drain()
                                v.nop().then_inc(yt_fr, 1)
                                v.nop().then_inc(o_v, 1)
                                v.reduce_sum(statb[:, t_:t_ + 1],
                                             r2b[:, ts4 * D:(ts4 + 1) * D], axis=AX.X)
                            v.wait_ge(o_sq2, (tt + 1) * 4)
                            v.drain()
                            c0, c1 = tt * 4, (tt + 1) * 4
                            v.tensor_scalar(nmeanb[:, c0:c1], statb[:, c0:c1],
                                            -1.0 / D, None, op0=ALU.mult)
                            v.tensor_scalar(stat2b[:, c0:c1], stat2b[:, c0:c1],
                                            1.0 / D, None, op0=ALU.mult)
                            v.drain()
                            v.tensor_tensor(lnsc[:, c0:c1], nmeanb[:, c0:c1],
                                            nmeanb[:, c0:c1], op=ALU.mult)
                            v.drain()
                            v.tensor_tensor(stat2b[:, c0:c1], stat2b[:, c0:c1],
                                            lnsc[:, c0:c1], op=ALU.subtract)
                            v.drain()
                            v.tensor_scalar(stat2b[:, c0:c1], stat2b[:, c0:c1],
                                            EPS_LN, None, op0=ALU.add)
                            v.drain()
                            v.reciprocal(stat2b[:, c0:c1], stat2b[:, c0:c1])
                            v.drain()
                            v.nop().then_inc(o_rec, 1)
                            v.wait_ge(o_rsq, tt + 1)
                            for ts4 in range(4):
                                t_ = tt * 4 + ts4
                                v.tensor_scalar(r2b[:, ts4 * D:(ts4 + 1) * D],
                                                r2b[:, ts4 * D:(ts4 + 1) * D],
                                                nmeanb[:, t_:t_ + 1],
                                                rstdb[:, t_:t_ + 1],
                                                op0=ALU.add, op1=ALU.mult
                                                ).then_inc(o_vv, 1)

                        quant(0)
                        for tt in range(1, 4):
                            quant(tt)
                            posttt(tt - 1)
                        posttt(3)

                    @blk.scalar
                    def _(s):
                        s.dma_start(out=r1m[:, 0:D],
                                    in_=r1_d.ap()[0:128, :]).then_inc(o_ld, 16)
                        s.dma_start(out=r1m[:, D:2 * D],
                                    in_=r1_d.ap()[128:256, :]).then_inc(o_ld, 16)
                        for tt in range(4):
                            if DEBUG:
                                s.wait_ge(q_v, (tt + 1) * 32)
                                s.dma_start(out=hqq_dbg[tt],
                                            in_=hqq[:, (tt % 2) * 32 * HALF:
                                                    (tt % 2 + 1) * 32 * HALF]
                                            ).then_inc(dbg_q, 16)
                                s.wait_ge(y_ev, (tt + 1) * 8)
                                s.dma_start(out=yf_dbg[tt], in_=yf[:]
                                            ).then_inc(dbg_y, 16)
                            for ts4 in range(4):
                                t_ = tt * 4 + ts4
                                s.wait_ge(o_v, t_ + 1)
                                s.activation(out=sq_scr[:],
                                             in_=r2b[:, ts4 * D:(ts4 + 1) * D],
                                             func=AF.Square,
                                             accum_out=stat2b[:, t_:t_ + 1]
                                             ).then_inc(o_sq2, 1)
                                if t_ + 2 < NT:
                                    tn = t_ + 2
                                    s.dma_start(out=r1m[:, (tn % 2) * D:(tn % 2 + 1) * D],
                                                in_=r1_d.ap()[tn * 128:(tn + 1) * 128, :]
                                                ).then_inc(o_ld, 16)
                            s.wait_ge(o_rec, tt + 1)
                            s.sqrt(rstdb[:, tt * 4:(tt + 1) * 4],
                                   stat2b[:, tt * 4:(tt + 1) * 4]).then_inc(o_rsq, 1)

                    @blk.tensor
                    def _(t):
                        for tt in range(4):
                            t.wait_ge(q_v, (tt + 1) * 32)
                            hq0 = (tt % 2) * 32 * HALF
                            for db in range(8):
                                i = tt * 8 + db
                                if i >= 4:
                                    t.wait_ge(y_ev, i - 3)
                                for fs in range(32):
                                    ins = t.matmul(
                                        ps_y[:, (i % 4) * HALF:(i % 4 + 1) * HALF],
                                        w2qt[:, fs * D + db * 128:fs * D + db * 128 + 128],
                                        hqq[:, hq0 + fs * HALF:hq0 + (fs + 1) * HALF],
                                        start=(fs == 0), stop=(fs == 31))
                                ins.then_inc(y_pe, 1)
                            t.wait_ge(y_ev, (tt + 1) * 8)
                            for ts4 in range(4):
                                t_ = tt * 4 + ts4
                                if t_ >= 2:
                                    t.wait_ge(yt_fr, t_ - 1)
                                for db in range(8):
                                    ins = t.transpose(
                                        ps_yt[:, (t_ % 2) * D + db * 128:
                                              (t_ % 2) * D + db * 128 + 128],
                                        yf[:, db * HALF + ts4 * 128:
                                           db * HALF + (ts4 + 1) * 128],
                                        idf[:])
                                ins.then_inc(yt_pe, 1)

        if DEBUG:
            d_st = sem("d_st")
            with nc.Block() as blk:
                @blk.sync
                def _(s):
                    s.dma_start(out=st_dbg[:, 0:16], in_=stat[:]).then_inc(d_st, 16)
                    s.dma_start(out=st_dbg[:, 16:32], in_=nmean[:]).then_inc(d_st, 16)
                    s.dma_start(out=st_dbg[:, 32:48], in_=rstd[:]).then_inc(d_st, 16)
                    s.dma_start(out=st_dbg[:, 48:64], in_=xmaxr[:]).then_inc(d_st, 16)
                    s.wait_ge(d_st, 64)

    return nc


def _host_tables(seg_start):
    j = np.arange(HALF, dtype=np.float64)
    scale_vec = (2.0 * j + 0.4 * D) / (1.4 * D)
    inv_freq = 1.0 / 10000.0 ** (j / HALF)
    c = seg_start + 512.0
    lg = np.log(np.float64(GAMMA))

    l_own = np.arange(T, dtype=np.float64)
    p_own = seg_start + l_own
    theta = inv_freq[:, None] * p_own[None, :]
    aq = scale_vec[:, None] ** ((p_own[None, :] - c) / 512.0)
    gq = np.exp((l_own - 1024.0) * lg)
    ak = scale_vec[:, None] ** (-(p_own[None, :] - c) / 512.0)
    gk = np.exp((1024.0 - l_own) * lg)
    cq = (np.cos(theta) * aq * gq[None, :]).astype(np.float32)
    sq = (np.sin(theta) * aq * gq[None, :]).astype(np.float32)
    ck = (np.cos(theta) * ak * gk[None, :]).astype(np.float32)
    sk = (np.sin(theta) * ak * gk[None, :]).astype(np.float32)

    lw = np.arange(-WRM, 0, dtype=np.float64)
    pw = seg_start + lw
    thw = pw[:, None] * inv_freq[None, :]
    akw = scale_vec[None, :] ** (-(pw[:, None] - c) / 512.0)
    gkw = np.exp((1024.0 - lw) * lg)
    ckw = (np.cos(thw) * akw * gkw[:, None]).astype(np.float32)
    skw = (np.sin(thw) * akw * gkw[:, None]).astype(np.float32)
    return cq, sq, ck, sk, ckw, skw


def prep_in_maps(L0_attn_X, L0_attn_W_Q, L0_attn_W_K, L0_attn_W_V,
                 L0_ffn_w1, L0_ffn_b1, L0_ffn_w2, L0_ffn_b2):
    X = np.asarray(L0_attn_X, dtype=np.float32)
    perm = np.concatenate([np.arange(0, D, 2), np.arange(1, D, 2)])
    wq = np.ascontiguousarray(np.asarray(L0_attn_W_Q, np.float32)[:, perm])
    wk = np.ascontiguousarray(np.asarray(L0_attn_W_K, np.float32)[:, perm])
    wv = np.ascontiguousarray(np.asarray(L0_attn_W_V, np.float32))
    w1 = np.ascontiguousarray(np.asarray(L0_ffn_w1, np.float32))
    w2 = np.ascontiguousarray(np.asarray(L0_ffn_w2, np.float32))
    b1t = np.ascontiguousarray(np.asarray(L0_ffn_b1, np.float32).reshape(32, 128).T)
    b2t = np.ascontiguousarray(np.asarray(L0_ffn_b2, np.float32).reshape(8, 128).T)
    tri = np.triu(np.ones((128, 128))).astype(ml_dtypes.bfloat16)
    idb = np.eye(128).astype(ml_dtypes.bfloat16)
    idf = np.eye(128, dtype=np.float32)

    tables = [_host_tables(seg * T) for seg in range(4)]
    zeros_warm = np.zeros((WRM, D), np.float32)

    in_maps = []
    for core in range(8):
        seq, seg = divmod(core, 4)
        cq, sq, ck, sk, ckw, skw = tables[seg]
        xo = np.ascontiguousarray(X[seq, seg * T:(seg + 1) * T, :])
        xw = (zeros_warm if seg == 0 else
              np.ascontiguousarray(X[seq, seg * T - WRM:seg * T, :]))
        in_maps.append(dict(
            x_own=xo, x_warm=xw, wq=wq, wk=wk, wv=wv, w1=w1, w2=w2,
            b1t=b1t, b2t=b2t, cq=cq, sq=sq, ck=ck, sk=sk, ckw=ckw, skw=skw,
            tri=tri, idb=idb, idf=idf))
    return in_maps


def kernel(L0_attn_X, L0_attn_W_Q, L0_attn_W_K, L0_attn_W_V,
           L0_ffn_w1, L0_ffn_b1, L0_ffn_w2, L0_ffn_b2):
    in_maps = prep_in_maps(L0_attn_X, L0_attn_W_Q, L0_attn_W_K, L0_attn_W_V,
                           L0_ffn_w1, L0_ffn_b1, L0_ffn_w2, L0_ffn_b2)
    nc = build()
    res = run_bass_kernel_spmd(nc, in_maps, core_ids=list(range(8)))
    out = np.empty((2, 8192, D), np.float32)
    for core in range(8):
        seq, seg = divmod(core, 4)
        out[seq, seg * T:(seg + 1) * T, :] = res.results[core]["out"]
    return out


# version 11
# speedup vs baseline: 1.0650x; 1.0650x over previous
import numpy as np
import ml_dtypes
from contextlib import ExitStack

import concourse.bass as bass
import concourse.mybir as mybir
from concourse.bass_utils import run_bass_kernel_spmd

F32 = mybir.dt.float32
BF16 = mybir.dt.bfloat16
I32 = mybir.dt.int32
AF = mybir.ActivationFunctionType
ALU = mybir.AluOpType
AX = mybir.AxisListType

D = 1024
HALF = 512
FF = 4096
T = 2048
WRM = 512
NW = WRM // 128
NT = T // 128
NG = 4
GAMMA = 0.96875
EPS_LN = 1e-5
EPS_Q = 1e-8
DEBUG = False
UPTO = "full"
_STAGES = ["warm", "attn", "f1", "f2a", "f2b", "g1", "g2"]


def _on(stage):
    return _STAGES.index(stage) <= _STAGES.index(UPTO) if UPTO != "full" else True


def pes_before(n):
    return n * (n + 1) // 2




def dma3(eng, out2d, in2d, n0, n1):
    return eng.dma_start(
        out=out2d.rearrange("p (a b) -> p a b", a=n0, b=n1),
        in_=in2d.rearrange("(a p) b -> p a b", p=128))

def build():
    nc = bass.Bass("TRN2", target_bir_lowering=False,
                   detect_race_conditions=False)

    x_own = nc.dram_tensor("x_own", [T, D], F32, kind="ExternalInput")
    x_warm = nc.dram_tensor("x_warm", [WRM, D], F32, kind="ExternalInput")
    wq_d = nc.dram_tensor("wq", [D, D], F32, kind="ExternalInput")
    wk_d = nc.dram_tensor("wk", [D, D], F32, kind="ExternalInput")
    wv_d = nc.dram_tensor("wv", [D, D], F32, kind="ExternalInput")
    w1_d = nc.dram_tensor("w1", [FF, D], F32, kind="ExternalInput")
    w2_d = nc.dram_tensor("w2", [D, FF], F32, kind="ExternalInput")
    b1_d = nc.dram_tensor("b1t", [128, 32], F32, kind="ExternalInput")
    b2_d = nc.dram_tensor("b2t", [128, 8], F32, kind="ExternalInput")
    cq_d = nc.dram_tensor("cq", [HALF, T], F32, kind="ExternalInput")
    sq_d = nc.dram_tensor("sq", [HALF, T], F32, kind="ExternalInput")
    ck_d = nc.dram_tensor("ck", [HALF, T], F32, kind="ExternalInput")
    sk_d = nc.dram_tensor("sk", [HALF, T], F32, kind="ExternalInput")
    ckw_d = nc.dram_tensor("ckw", [WRM, HALF], F32, kind="ExternalInput")
    skw_d = nc.dram_tensor("skw", [WRM, HALF], F32, kind="ExternalInput")
    tri_d = nc.dram_tensor("tri", [128, 128], BF16, kind="ExternalInput")
    idb_d = nc.dram_tensor("idb", [128, 128], BF16, kind="ExternalInput")
    idf_d = nc.dram_tensor("idf", [128, 128], F32, kind="ExternalInput")
    out_d = nc.dram_tensor("out", [T, D], F32, kind="ExternalOutput")

    dbg = "ExternalOutput" if DEBUG else "Internal"
    r1_d = nc.dram_tensor("r1_spill", [T, D], F32, kind=dbg)
    h_d = nc.dram_tensor("h_spill", [4, FF, 512], F32, kind=dbg)
    pm_d = nc.dram_tensor("pm_bounce", [128], F32)
    cc1_in = nc.dram_tensor("cc1_in", [1], F32)
    cc1_out = nc.dram_tensor("cc1_out", [1], F32)
    cc2_in = nc.dram_tensor("cc2_in", [1], F32)
    cc2_out = nc.dram_tensor("cc2_out", [1], F32)
    if DEBUG:
        kx_dbg = nc.dram_tensor("kx_dbg", [128, 8 * T], BF16, kind="ExternalOutput")
        vt_dbg = nc.dram_tensor("vt_dbg", [128, NT * D], BF16, kind="ExternalOutput")
        sin_dbg = nc.dram_tensor("sin_dbg", [128, 8 * D], BF16, kind="ExternalOutput")
        st_dbg = nc.dram_tensor("st_dbg", [128, 64], F32, kind="ExternalOutput")
        hqq_dbg = nc.dram_tensor("hqq_dbg", [4, 128, 32 * HALF], BF16,
                                 kind="ExternalOutput")
        yf_dbg = nc.dram_tensor("yf_dbg", [4, 128, 8 * HALF], F32,
                                kind="ExternalOutput")

    with ExitStack() as top:
        SB = lambda n, s, d: top.enter_context(nc.sbuf_tensor(n, s, d))
        sem = lambda n: top.enter_context(nc.semaphore(n))

        tri = SB("tri_sb", [128, 128], BF16)
        idb = SB("idb_sb", [128, 128], BF16)
        idf = SB("idf_sb", [128, 128], F32)
        b1t = SB("b1t_sb", [128, 32], F32)
        b2t = SB("b2t_sb", [128, 8], F32)
        stat = SB("stat_sb", [128, 16], F32)
        stat2 = SB("stat2_sb", [128, 16], F32)
        nmean = SB("nmean_sb", [128, 16], F32)
        rstd = SB("rstd_sb", [128, 16], F32)
        xmaxr = SB("xmaxr_sb", [128, 16], F32)
        statb = SB("statb_sb", [128, 16], F32)
        stat2b = SB("stat2b_sb", [128, 16], F32)
        nmeanb = SB("nmeanb_sb", [128, 16], F32)
        rstdb = SB("rstdb_sb", [128, 16], F32)
        sq_scr = SB("sq_scr", [128, D], F32)
        redrow = SB("redrow", [1, 128], F32)
        redsc = SB("redsc", [1, 1], F32)
        gmax1 = SB("gmax1", [128, 1], F32)
        sinsc = SB("sinsc", [128, 1], F32)
        rinv1 = SB("rinv1", [128, 1], F32)
        gmax2 = SB("gmax2", [128, 1], F32)
        shsc = SB("shsc", [128, 1], F32)
        hrec = SB("hrec", [128, 1], F32)
        s1v = SB("s1v", [128, 32], F32)
        qv2 = SB("qv2", [128, 32], F32)
        s2v = SB("s2v", [128, 8], F32)
        b1i = SB("b1i", [128, 32], F32)
        b2i = SB("b2i", [128, 8], F32)
        am1 = SB("am1", [128, 32], F32)
        am2 = SB("am2", [128, 8], F32)
        am2h = SB("am2h", [128, 4], F32)
        rc1 = SB("rc1", [128, 32], F32)
        rc2 = SB("rc2", [128, 8], F32)
        hrm4 = SB("hrm4", [128, 128], F32)
        hrm = SB("hrm", [128, 32], F32)
        hsc = SB("hsc", [128, 32], F32)
        lnsc = SB("lnsc", [128, 16], F32)
        xm1 = SB("xm1", [128, 1], F32)
        hm1 = SB("hm1", [128, 1], F32)
        ivec = SB("ivec", [128, 32], I32)

        with ExitStack() as scopeA:
            SA = lambda n, s, d: scopeA.enter_context(nc.sbuf_tensor(n, s, d))
            wqkv = SA("wqkv", [128, 3 * 8 * D], BF16)
            sin_sb = SA("sin_sb", [128, 8 * D], BF16)

            w_ld = sem("w_ld"); w_lg = sem("w_lg"); w_pet = sem("w_pet")
            w_evt = sem("w_evt"); w_pek = sem("w_pek"); w_evk = sem("w_evk")
            w_vx = sem("w_vx"); w_pes = sem("w_pes"); w_evs = sem("w_evs")

            with ExitStack() as wctx:
                WSB = lambda n, s, d: wctx.enter_context(nc.sbuf_tensor(n, s, d))
                xwbf = WSB("xwbf", [128, NW * D], BF16)
                xwfm = WSB("xwfm", [128, 8 * WRM], BF16)
                kxw = WSB("kxw", [128, NW * D], BF16)
                vw = WSB("vw", [128, NW * D], BF16)
                kwraw = WSB("kwraw", [128, 2 * D], BF16)
                twbl = WSB("twbl", [128, 2 * D], F32)
                xs1 = WSB("xs1", [128, HALF], F32)
                xs2 = WSB("xs2", [128, HALF], F32)

                with ExitStack() as pctx:
                    ps_bf = pctx.enter_context(
                        nc.psum_tensor("ps_bf", [128, 2 * 1024], BF16))
                    ps_f = pctx.enter_context(
                        nc.psum_tensor("ps_f", [128, 4 * HALF], F32))

                    with nc.Block() as blk:
                        @blk.sync
                        def _(s):
                            for t_, d_ in ((tri, tri_d), (idb, idb_d), (idf, idf_d),
                                           (b1t, b1_d), (b2t, b2_d)):
                                s.dma_start(out=t_[:], in_=d_[:]).then_inc(w_ld, 16)
                            for m in range(NW):
                                p = m % 2
                                if m >= 2:
                                    s.wait_ge(w_vx, m - 1)
                                s.dma_start(out=twbl[:, p * D:p * D + HALF],
                                            in_=ckw_d[m * 128:(m + 1) * 128, :]
                                            ).then_inc(w_ld, 16)
                                s.dma_start(out=twbl[:, p * D + HALF:(p + 1) * D],
                                            in_=skw_d[m * 128:(m + 1) * 128, :]
                                            ).then_inc(w_ld, 16)

                        @blk.gpsimd
                        def _(g):
                            dma3(g, xwbf[:], x_warm.ap(), NW, D).then_inc(w_lg, 16)
                            for i, wd in enumerate((wq_d, wk_d, wv_d)):
                                dma3(g, wqkv[:, i * 8 * D:(i + 1) * 8 * D],
                                     wd.ap(), 8, D).then_inc(w_lg, 16)

                        @blk.tensor
                        def _(t):
                            t.wait_ge(w_lg, 64)
                            t.wait_ge(w_ld, 144)
                            for d in range(8):
                                p = d % 2
                                if d >= 2:
                                    t.wait_ge(w_evt, d - 1)
                                for wt in range(NW):
                                    ins = t.transpose(
                                        ps_bf[:, p * 1024 + wt * 128:
                                              p * 1024 + (wt + 1) * 128],
                                        xwbf[:, wt * D + d * 128:wt * D + (d + 1) * 128],
                                        idb[:])
                                ins.then_inc(w_pet, 1)
                            t.wait_ge(w_evt, 8)
                            i = 0
                            for m in range(NW):
                                for which in (1, 2):
                                    for hv in range(2):
                                        slot = i % 4
                                        if i >= 4:
                                            t.wait_ge(w_evk, i - 3)
                                        for d in range(8):
                                            ins = t.matmul(
                                                ps_f[:, slot * HALF:(slot + 1) * HALF],
                                                xwfm[:, d * WRM + m * 128:
                                                     d * WRM + (m + 1) * 128],
                                                wqkv[:, which * 8 * D + d * D + hv * HALF:
                                                     which * 8 * D + d * D + hv * HALF + HALF],
                                                start=(d == 0), stop=(d == 7))
                                        ins.then_inc(w_pek, 1)
                                        i += 1

                        @blk.scalar
                        def _(s):
                            for d in range(8):
                                p = d % 2
                                s.wait_ge(w_pet, d + 1)
                                s.copy(out=xwfm[:, d * WRM:(d + 1) * WRM],
                                       in_=ps_bf[:, p * 1024:p * 1024 + WRM]
                                       ).then_inc(w_evt, 1)
                            i = 0
                            for m in range(NW):
                                for which in (1, 2):
                                    for hv in range(2):
                                        slot = i % 4
                                        s.wait_ge(w_pek, i + 1)
                                        if which == 1:
                                            if m >= 2:
                                                s.wait_ge(w_vx, m - 1)
                                            dst = kwraw[:, (m % 2) * D + hv * HALF:
                                                        (m % 2) * D + hv * HALF + HALF]
                                        else:
                                            dst = vw[:, m * D + hv * HALF:
                                                     m * D + hv * HALF + HALF]
                                        s.copy(out=dst,
                                               in_=ps_f[:, slot * HALF:(slot + 1) * HALF]
                                               ).then_inc(w_evk, 1)
                                        i += 1

                        @blk.vector
                        def _(v):
                            for m in range(NW):
                                p = m % 2
                                v.wait_ge(w_evk, 4 * m + 2)
                                v.wait_ge(w_ld, 80 + 32 * min(NW, m + 2))
                                cw = twbl[:, p * D:p * D + HALF]
                                sw = twbl[:, p * D + HALF:(p + 1) * D]
                                k1 = kwraw[:, (m % 2) * D:(m % 2) * D + HALF]
                                k2 = kwraw[:, (m % 2) * D + HALF:(m % 2) * D + D]
                                v.tensor_tensor(xs1[:], k1, cw, op=ALU.mult)
                                v.tensor_tensor(xs2[:], k2, sw, op=ALU.mult)
                                v.tensor_tensor(kxw[:, m * D:m * D + HALF],
                                                xs1[:], xs2[:], op=ALU.subtract)
                                v.tensor_tensor(xs1[:], k2, cw, op=ALU.mult)
                                v.tensor_tensor(xs2[:], k1, sw, op=ALU.mult)
                                v.tensor_tensor(kxw[:, m * D + HALF:(m + 1) * D],
                                                xs1[:], xs2[:], op=ALU.add
                                                ).then_inc(w_vx, 1)

                with ExitStack() as pctx:
                    ps_si = pctx.enter_context(
                        nc.psum_tensor("ps_si", [128, 8 * HALF], F32))

                    with nc.Block() as blk:
                        @blk.tensor
                        def _(t):
                            for pas in range(2):
                                if pas == 1:
                                    t.wait_ge(w_evs, 8)
                                for m in range(NW):
                                    for dk in range(4):
                                        for hv in range(2):
                                            ins = t.matmul(
                                                ps_si[:, (dk * 2 + hv) * HALF:
                                                      (dk * 2 + hv + 1) * HALF],
                                                kxw[:, m * D + (pas * 4 + dk) * 128:
                                                    m * D + (pas * 4 + dk) * 128 + 128],
                                                vw[:, m * D + hv * HALF:
                                                   m * D + hv * HALF + HALF],
                                                start=(m == 0), stop=(m == NW - 1))
                                ins.then_inc(w_pes, 1)

                        @blk.scalar
                        def _(s):
                            for pas in range(2):
                                s.wait_ge(w_pes, pas + 1)
                                for dk in range(4):
                                    for hv in range(2):
                                        s.copy(out=sin_sb[:, (pas * 4 + dk) * D + hv * HALF:
                                                          (pas * 4 + dk) * D + hv * HALF + HALF],
                                               in_=ps_si[:, (dk * 2 + hv) * HALF:
                                                         (dk * 2 + hv + 1) * HALF]
                                               ).then_inc(w_evs, 1)

            a_lg = sem("a_lg"); a_tx = sem("a_tx"); a_pet = sem("a_pet")
            a_evt = sem("a_evt"); a_tbl = sem("a_tbl"); a_vt = sem("a_vt")
            a_pea = sem("a_pea"); a_eva = sem("a_eva"); a_vxa = sem("a_vxa")
            a_pes = sem("a_pes"); a_evs = sem("a_evs"); a_peo = sem("a_peo")
            a_pfo = sem("a_pfo"); a_ldr = sem("a_ldr"); a_str = sem("a_str")
            a_sq = sem("a_sq")
            PEA_G = 24

            if not _on("attn"):
                return nc
            with ExitStack() as actx:
                ASB = lambda n, s, d: actx.enter_context(nc.sbuf_tensor(n, s, d))
                kx = ASB("kx", [128, 8 * T], BF16)
                vt = ASB("vt", [128, NT * D], BF16)
                xbf = ASB("xbf", [128, 4 * D], BF16)
                xfm = ASB("xfm", [128, 8 * HALF], BF16)
                qx = ASB("qx", [128, 8 * HALF], BF16)
                raw = ASB("raw", [128, 8 * HALF], BF16)
                tblc = ASB("tblc", [128, 4 * HALF], F32)
                tbls = ASB("tbls", [128, 4 * HALF], F32)
                scb = ASB("scb", [128, 2 * 128], BF16)
                xr = ASB("xr", [128, D], F32)
                r1b = ASB("r1b", [128, 2 * D], F32)
                ax1 = ASB("ax1", [128, HALF], F32)
                ax2 = ASB("ax2", [128, HALF], F32)
                ps_main = actx.enter_context(
                    nc.psum_tensor("ps_main", [128, 4 * HALF], F32))
                ps_sc = actx.enter_context(
                    nc.psum_tensor("ps_sc", [128, 2 * HALF], F32))
                ps_bf2 = actx.enter_context(
                    nc.psum_tensor("ps_bf2", [128, 2 * 1024], BF16))

                with nc.Block() as blk:
                    @blk.gpsimd
                    def _(g_):
                        for g in range(NG):
                            if g >= 1:
                                g_.wait_ge(a_tx, g)
                            sl = 0
                            src = x_own.ap()[g * 512:(g + 1) * 512, :]
                            dma3(g_, xbf[:, sl:sl + 4 * D], src, 4, D).then_inc(a_lg, 16)

                    @blk.sync
                    def _(s):
                        for g in range(NG):
                            if g >= 1:
                                s.wait_ge(a_vxa, g * 8)
                            for tb, dram in ((tblc, ck_d), (tbls, sk_d)):
                                dma3(s, tb[:], dram.ap()[:, g * HALF:(g + 1) * HALF],
                                     4, HALF).then_inc(a_tbl, 16)
                            s.wait_ge(a_vxa, g * 8 + 4)
                            for tb, dram in ((tblc, cq_d), (tbls, sq_d)):
                                dma3(s, tb[:], dram.ap()[:, g * HALF:(g + 1) * HALF],
                                     4, HALF).then_inc(a_tbl, 16)
                            for nn in range(4):
                                n = g * 4 + nn
                                if n >= 1:
                                    s.wait_ge(a_vt, n)
                                s.dma_start(out=xr[:],
                                            in_=x_own.ap()[n * 128:(n + 1) * 128, :]
                                            ).then_inc(a_ldr, 16)
                                s.wait_ge(a_vt, n + 1)
                                s.dma_start(out=r1_d.ap()[n * 128:(n + 1) * 128, :],
                                            in_=r1b[:, (n % 2) * D:(n % 2 + 1) * D]
                                            ).then_inc(a_str, 16)
                        s.wait_ge(a_str, 16 * NT)

                    @blk.tensor
                    def _(t):
                        for g in range(NG):
                            t.wait_ge(a_lg, (g + 1) * 16)
                            for d in range(8):
                                p = d % 2
                                if g * 8 + d >= 2:
                                    t.wait_ge(a_evt, g * 8 + d - 1)
                                xsl = 0
                                for wt in range(4):
                                    ins = t.transpose(
                                        ps_bf2[:, p * 1024 + wt * 128:
                                               p * 1024 + (wt + 1) * 128],
                                        xbf[:, xsl + wt * D + d * 128:
                                            xsl + wt * D + (d + 1) * 128],
                                        idb[:])
                                ins.then_inc(a_pet, 1)
                            t.nop().then_inc(a_tx, 1)
                            t.wait_ge(a_evt, (g + 1) * 8)
                            t.wait_ge(a_pfo, g * 4)
                            i0 = g * PEA_G
                            for fc in range(8):
                                i = i0 + fc
                                if i >= 4:
                                    t.wait_ge(a_eva, i - 3)
                                for d in range(8):
                                    ins = t.matmul(
                                        ps_main[:, (i % 4) * HALF:(i % 4 + 1) * HALF],
                                        wqkv[:, 8 * D + d * D + fc * 128:
                                             8 * D + d * D + fc * 128 + 128],
                                        xfm[:, d * HALF:(d + 1) * HALF],
                                        start=(d == 0), stop=(d == 7))
                                ins.then_inc(a_pea, 1)
                            for fc in range(8):
                                i = i0 + 8 + fc
                                t.wait_ge(a_eva, i - 3)
                                for d in range(8):
                                    ins = t.matmul(
                                        ps_main[:, (i % 4) * HALF:(i % 4 + 1) * HALF],
                                        wqkv[:, d * D + fc * 128:d * D + fc * 128 + 128],
                                        xfm[:, d * HALF:(d + 1) * HALF],
                                        start=(d == 0), stop=(d == 7))
                                ins.then_inc(a_pea, 1)
                            for tb in range(4):
                                for hv in range(2):
                                    i = i0 + 16 + tb * 2 + hv
                                    t.wait_ge(a_eva, i - 3)
                                    for d in range(8):
                                        ins = t.matmul(
                                            ps_main[:, (i % 4) * HALF:(i % 4 + 1) * HALF],
                                            xfm[:, d * HALF + tb * 128:
                                                d * HALF + (tb + 1) * 128],
                                            wqkv[:, 16 * D + d * D + hv * HALF:
                                                 16 * D + d * D + hv * HALF + HALF],
                                            start=(d == 0), stop=(d == 7))
                                    ins.then_inc(a_pea, 1)
                            t.wait_ge(a_vxa, g * 8 + 8)
                            t.wait_ge(a_eva, (g + 1) * PEA_G)
                            for nn in range(4):
                                n = g * 4 + nn
                                po = (n % 2) * D
                                if n >= 2:
                                    t.wait_ge(a_pfo, n - 1)
                                for d in range(8):
                                    for hv in range(2):
                                        t.matmul(
                                            ps_main[:, po + hv * HALF:po + hv * HALF + HALF],
                                            qx[:, d * HALF + nn * 128:
                                               d * HALF + nn * 128 + 128],
                                            sin_sb[:, d * D + hv * HALF:
                                                   d * D + hv * HALF + HALF],
                                            start=(d == 0), stop=False)
                                base_s = pes_before(n)
                                for m in range(n + 1):
                                    sl_s = m % 2
                                    if m >= 2:
                                        t.wait_ge(a_evs, base_s + m - 1)
                                    for d in range(8):
                                        ins = t.matmul(
                                            ps_sc[:, sl_s * HALF:sl_s * HALF + 128],
                                            kx[:, d * T + m * 128:d * T + m * 128 + 128],
                                            qx[:, d * HALF + nn * 128:
                                               d * HALF + nn * 128 + 128],
                                            start=(d == 0), stop=(d == 7))
                                    ins.then_inc(a_pes, 1)
                                    if m > 0:
                                        t.wait_ge(a_evs, base_s + m)
                                        for hv in range(2):
                                            t.matmul(
                                                ps_main[:, po + hv * HALF:
                                                        po + hv * HALF + HALF],
                                                scb[:, ((m - 1) % 2) * 128:
                                                    ((m - 1) % 2 + 1) * 128],
                                                vt[:, (m - 1) * D + hv * HALF:
                                                   (m - 1) * D + hv * HALF + HALF],
                                                start=False, stop=False)
                                t.wait_ge(a_evs, base_s + n + 1)
                                for hv in range(2):
                                    ins = t.matmul(
                                        ps_main[:, po + hv * HALF:po + hv * HALF + HALF],
                                        scb[:, (n % 2) * 128:(n % 2 + 1) * 128],
                                        vt[:, n * D + hv * HALF:n * D + hv * HALF + HALF],
                                        start=False, stop=True)
                                ins.then_inc(a_peo, 1)

                    @blk.scalar
                    def _(s):
                        for g in range(NG):
                            for d in range(8):
                                p = d % 2
                                s.wait_ge(a_pet, g * 8 + d + 1)
                                s.copy(out=xfm[:, d * HALF:(d + 1) * HALF],
                                       in_=ps_bf2[:, p * 1024:p * 1024 + HALF]
                                       ).then_inc(a_evt, 1)
                            i0 = g * PEA_G
                            for j in range(PEA_G):
                                i = i0 + j
                                s.wait_ge(a_pea, i + 1)
                                slot = i % 4
                                if j < 8:
                                    if j == 0:
                                        s.wait_ge(a_vxa, g * 8)
                                    dst = raw[:, j * HALF:(j + 1) * HALF]
                                elif j < 16:
                                    if j == 8:
                                        s.wait_ge(a_vxa, g * 8 + 4)
                                    dst = raw[:, (j - 8) * HALF:(j - 7) * HALF]
                                else:
                                    tb, hv = divmod(j - 16, 2)
                                    tg = g * 4 + tb
                                    dst = vt[:, tg * D + hv * HALF:tg * D + hv * HALF + HALF]
                                s.copy(out=dst,
                                       in_=ps_main[:, slot * HALF:(slot + 1) * HALF]
                                       ).then_inc(a_eva, 1)
                            for nn in range(4):
                                n = g * 4 + nn
                                s.wait_ge(a_vt, n + 1)
                                s.activation(out=sq_scr[:],
                                             in_=r1b[:, (n % 2) * D:(n % 2 + 1) * D],
                                             func=AF.Square,
                                             accum_out=stat2[:, n:n + 1]).then_inc(a_sq, 1)

                    @blk.vector
                    def _(v):
                        v.wait_ge(w_ld, 80)
                        for g in range(NG):
                            for which in range(2):
                                for j in range(4):
                                    v.wait_ge(a_eva, g * PEA_G + which * 8 + j + 5)
                                    v.wait_ge(a_tbl, g * 64 + 32 + which * 32)
                                    cw = tblc[:, j * HALF:(j + 1) * HALF]
                                    sw = tbls[:, j * HALF:(j + 1) * HALF]
                                    r1_ = raw[:, j * HALF:(j + 1) * HALF]
                                    r2_ = raw[:, (j + 4) * HALF:(j + 5) * HALF]
                                    if which == 0:
                                        o1 = kx[:, j * T + g * HALF:j * T + (g + 1) * HALF]
                                        o2 = kx[:, (j + 4) * T + g * HALF:
                                                (j + 4) * T + (g + 1) * HALF]
                                    else:
                                        o1 = qx[:, j * HALF:(j + 1) * HALF]
                                        o2 = qx[:, (j + 4) * HALF:(j + 5) * HALF]
                                    v.tensor_tensor(ax1[:], r1_, cw, op=ALU.mult)
                                    v.tensor_tensor(ax2[:], r2_, sw, op=ALU.mult)
                                    v.tensor_tensor(o1, ax1[:], ax2[:], op=ALU.subtract)
                                    v.tensor_tensor(ax1[:], r2_, cw, op=ALU.mult)
                                    v.tensor_tensor(ax2[:], r1_, sw, op=ALU.mult)
                                    v.tensor_tensor(o2, ax1[:], ax2[:], op=ALU.add
                                                    ).then_inc(a_vxa, 1)
                            for nn in range(4):
                                n = g * 4 + nn
                                base_s = pes_before(n)
                                for m in range(n + 1):
                                    v.wait_ge(a_pes, base_s + m + 1)
                                    sl_s = m % 2
                                    if m == n:
                                        v.tensor_tensor(
                                            scb[:, sl_s * 128:(sl_s + 1) * 128],
                                            ps_sc[:, sl_s * HALF:sl_s * HALF + 128],
                                            tri[:], op=ALU.mult).then_inc(a_evs, 1)
                                    else:
                                        v.tensor_copy(
                                            scb[:, sl_s * 128:(sl_s + 1) * 128],
                                            ps_sc[:, sl_s * HALF:sl_s * HALF + 128]
                                            ).then_inc(a_evs, 1)
                                po = (n % 2) * D
                                v.wait_ge(a_peo, n + 1)
                                v.wait_ge(a_ldr, (n + 1) * 16)
                                if n >= 2:
                                    v.wait_ge(a_str, n * 16)
                                    v.wait_ge(a_sq, n - 1)
                                v.tensor_tensor(r1b[:, (n % 2) * D:(n % 2 + 1) * D],
                                                ps_main[:, po:po + D], xr[:],
                                                op=ALU.add).then_inc(a_vt, 1)
                                v.nop().then_inc(a_pfo, 1)
                                r1_ = r1b[:, (n % 2) * D:(n % 2 + 1) * D]
                                v.reduce_sum(stat[:, n:n + 1], r1_, axis=AX.X)
                                v.reduce_max(xmaxr[:, n:n + 1], r1_, axis=AX.X)

        r_ld = sem("r_ld"); r_lg = sem("r_lg"); r_q = sem("r_q")
        r_pe = sem("r_pe"); r_ev = sem("r_ev"); r_vx = sem("r_vx")
        r_cc = sem("r_cc"); r_sq = sem("r_sq")
        wsl = [sem(f"wsl{k}") for k in range(4)]
        rsl = [sem(f"rsl{k}") for k in range(4)]
        x_q = sem("x_q"); x_pe = sem("x_pe"); x_ev = sem("x_ev")
        m_pe = sem("m_pe"); m_ev = sem("m_ev"); m_vr = sem("m_vr")
        msl = [sem(f"msl{k}") for k in range(4)]

        if not _on("f1"):
            return nc
        with ExitStack() as scopeF:
            SF = lambda n, s, d: scopeF.enter_context(nc.sbuf_tensor(n, s, d))
            w1qt = SF("w1qt", [128, 8 * FF], BF16)
            xqfm = SF("xqfm", [128, 8 * T], BF16)

            with ExitStack() as scopeF1:
                SF1 = lambda n, s, d: scopeF1.enter_context(nc.sbuf_tensor(n, s, d))
                w1c = SF1("w1c", [128, 4 * D], F32)
                w1i = SF1("w1i", [128, D], I32)
                w1b = SF1("w1b", [128, 2 * D], BF16)
                ps_w = scopeF1.enter_context(
                    nc.psum_tensor("ps_w", [128, 8 * D], BF16))

                with nc.Block() as blk:
                    @blk.sync
                    def _(s):
                        for fb in range(32):
                            if fb >= 4:
                                s.wait_ge(r_q, fb - 3)
                            s.dma_start(out=w1c[:, (fb % 4) * D:(fb % 4 + 1) * D],
                                        in_=w1_d.ap()[fb * 128:(fb + 1) * 128, :]
                                        ).then_inc(wsl[fb % 4], 16)

                    @blk.gpsimd
                    def _(g):
                        g.wait_ge(r_vx, 2)
                        g.dma_start(out=pm_d[:], in_=xm1[:, 0]).then_inc(r_lg, 16)
                        g.wait_ge(r_lg, 16)
                        g.dma_start(out=redrow[:], in_=pm_d[None, :]).then_inc(r_lg, 16)
                        g.wait_ge(r_vx, 3)
                        g.dma_start(out=cc1_in[:], in_=redsc[0, :]).then_inc(r_lg, 16)
                        g.wait_ge(r_lg, 48)
                        g.collective_compute(
                            "AllReduce", ALU.max,
                            replica_groups=[list(range(8))],
                            ins=[cc1_in.ap()], outs=[cc1_out.ap()],
                        ).then_inc(r_cc, 1)
                        g.wait_ge(r_cc, 1)
                        g.dma_start(out=gmax1[:],
                                    in_=cc1_out.ap()[None, :].broadcast_to((128, 1))
                                    ).then_inc(r_lg, 16)

                    @blk.vector
                    def _(v):
                        v.drain()
                        v.tensor_scalar(nmean[:], stat[:], -1.0 / D, None, op0=ALU.mult)
                        v.tensor_scalar(stat2[:], stat2[:], 1.0 / D, None, op0=ALU.mult)
                        v.drain()
                        v.tensor_tensor(lnsc[:], nmean[:], nmean[:], op=ALU.mult)
                        v.drain()
                        v.tensor_tensor(stat2[:], stat2[:], lnsc[:], op=ALU.subtract)
                        v.drain()
                        v.tensor_scalar(stat2[:], stat2[:], EPS_LN, None, op0=ALU.add)
                        v.drain()
                        v.reciprocal(stat2[:], stat2[:])
                        v.drain()
                        v.nop().then_inc(r_vx, 1)
                        v.wait_ge(r_sq, 1)
                        v.tensor_tensor(lnsc[:], xmaxr[:], nmean[:], op=ALU.add)
                        v.drain()
                        v.tensor_tensor(lnsc[:], lnsc[:], rstd[:], op=ALU.mult)
                        v.drain()
                        v.reduce_max(xm1[:], lnsc[:], axis=AX.X)
                        v.drain()
                        v.nop().then_inc(r_vx, 1)

                        def w1red(fb):
                            v.wait_ge(wsl[fb % 4], 16 * (fb // 4 + 1))
                            v.tensor_reduce(am1[:, fb:fb + 1],
                                            w1c[:, (fb % 4) * D:(fb % 4 + 1) * D],
                                            axis=AX.X, op=ALU.max,
                                            apply_absolute_value=True)

                        def w1q(fb):
                            v.tensor_scalar(am1[:, fb:fb + 1], am1[:, fb:fb + 1],
                                            EPS_Q, None, op0=ALU.max)
                            v.drain()
                            v.reciprocal(rc1[:, fb:fb + 1], am1[:, fb:fb + 1])
                            v.drain()
                            v.tensor_scalar(w1i[:], w1c[:, (fb % 4) * D:(fb % 4 + 1) * D],
                                            rc1[:, fb:fb + 1], 127.0,
                                            op0=ALU.mult, op1=ALU.mult)
                            if fb >= 2:
                                v.wait_ge(r_pe, fb - 1)
                            v.tensor_copy(w1b[:, (fb % 2) * D:(fb % 2 + 1) * D],
                                          w1i[:]).then_inc(r_q, 1)
                        w1red(0)
                        for fb in range(8):
                            w1red(fb + 1)
                            w1q(fb)
                        v.wait_ge(r_lg, 32)
                        v.reduce_max(redsc[:], redrow[:], axis=AX.X)
                        v.drain()
                        v.nop().then_inc(r_vx, 1)
                        for fb in range(8, 32):
                            if fb + 1 < 32:
                                w1red(fb + 1)
                            w1q(fb)
                        v.wait_ge(r_lg, 64)
                        v.tensor_scalar(gmax1[:], gmax1[:], EPS_Q, None, op0=ALU.max)
                        v.drain()
                        v.tensor_scalar(sinsc[:], gmax1[:], 1.0 / 255.0, None,
                                        op0=ALU.mult)
                        v.drain()
                        v.reciprocal(rinv1[:], sinsc[:])
                        v.tensor_scalar(s1v[:], am1[:], 1.0 / 127.0, None, op0=ALU.mult)
                        v.drain()
                        v.tensor_scalar(s1v[:], s1v[:], sinsc[:, 0:1], None, op0=ALU.mult)
                        v.drain()
                        v.reciprocal(lnsc[:], s1v[:, 0:16])
                        v.drain()
                        v.reciprocal(hrm4[:, 0:16], s1v[:, 16:32])
                        v.drain()
                        v.tensor_tensor(hsc[:, 0:16], b1t[:, 0:16], lnsc[:],
                                        op=ALU.mult)
                        v.tensor_tensor(hsc[:, 16:32], b1t[:, 16:32], hrm4[:, 0:16],
                                        op=ALU.mult)
                        v.drain()
                        v.tensor_copy(ivec[:], hsc[:])
                        v.drain()
                        v.tensor_copy(b1i[:], ivec[:])
                        v.drain()
                        v.nop().then_inc(r_vx, 1)

                    @blk.scalar
                    def _(s):
                        s.wait_ge(r_vx, 1)
                        s.sqrt(rstd[:], stat2[:]).then_inc(r_sq, 1)
                        for G in range(4):
                            s.wait_ge(r_pe, (G + 1) * 8)
                            for d in range(8):
                                s.copy(out=w1qt[:, d * FF + G * D:d * FF + (G + 1) * D],
                                       in_=ps_w[:, d * D:(d + 1) * D]).then_inc(r_ev, 1)

                    @blk.tensor
                    def _(t):
                        for fb in range(32):
                            G = fb // 8
                            t.wait_ge(r_q, fb + 1)
                            if G > 0 and fb % 8 == 0:
                                t.wait_ge(r_ev, G * 8)
                            for d in range(8):
                                ins = t.transpose(
                                    ps_w[:, d * D + (fb % 8) * 128:
                                         d * D + (fb % 8) * 128 + 128],
                                    w1b[:, (fb % 2) * D + d * 128:
                                        (fb % 2) * D + (d + 1) * 128],
                                    idb[:])
                            ins.then_inc(r_pe, 1)

            if not _on("f2a"):
                return nc
            with ExitStack() as scopeF2:
                SF2 = lambda n, s, d: scopeF2.enter_context(nc.sbuf_tensor(n, s, d))
                r1rl = SF2("r1rl", [128, 4 * D], F32)
                xqf = SF2("xqf", [128, 2 * D], F32)
                xqi = SF2("xqi", [128, 2 * D], I32)
                xqb = SF2("xqb", [128, 2 * D], BF16)

                with ExitStack() as pctx:
                    ps_w2 = pctx.enter_context(
                        nc.psum_tensor("ps_w2", [128, 8 * D], BF16))

                    with nc.Block() as blk:
                        @blk.sync
                        def _(s):
                            for t_ in range(NT):
                                if t_ < 4:
                                    s.wait_ge(r_q, 29 + t_)
                                else:
                                    s.wait_ge(x_q, t_ - 3)
                                s.dma_start(out=r1rl[:, (t_ % 4) * D:(t_ % 4 + 1) * D],
                                            in_=r1_d.ap()[t_ * 128:(t_ + 1) * 128, :]
                                            ).then_inc(rsl[t_ % 4], 16)

                        @blk.vector
                        def _(v):
                            for t_ in range(NT):
                                p = (t_ % 2) * D
                                v.wait_ge(rsl[t_ % 4], 16 * (t_ // 4 + 1))
                                src = r1rl[:, (t_ % 4) * D:(t_ % 4 + 1) * D]
                                v.tensor_scalar(xqf[:, p:p + D], src, nmean[:, t_:t_ + 1],
                                                rstd[:, t_:t_ + 1],
                                                op0=ALU.add, op1=ALU.mult)
                                v.tensor_scalar(xqf[:, p:p + D], xqf[:, p:p + D],
                                                rinv1[:, 0:1], 0.0,
                                                op0=ALU.mult, op1=ALU.max)
                                v.tensor_copy(xqi[:, p:p + D], xqf[:, p:p + D])
                                if t_ >= 2:
                                    v.wait_ge(x_pe, t_ - 1)
                                v.tensor_copy(xqb[:, p:p + D],
                                              xqi[:, p:p + D]).then_inc(x_q, 1)

                        @blk.tensor
                        def _(t):
                            for t_ in range(NT):
                                G = t_ // 8
                                t.wait_ge(x_q, t_ + 1)
                                if G > 0 and t_ % 8 == 0:
                                    t.wait_ge(x_ev, G * 8)
                                for d in range(8):
                                    ins = t.transpose(
                                        ps_w2[:, d * D + (t_ % 8) * 128:
                                              d * D + (t_ % 8) * 128 + 128],
                                        xqb[:, (t_ % 2) * D + d * 128:
                                            (t_ % 2) * D + (d + 1) * 128],
                                        idb[:])
                                ins.then_inc(x_pe, 1)

                        @blk.scalar
                        def _(s):
                            for G in range(2):
                                s.wait_ge(x_pe, (G + 1) * 8)
                                for d in range(8):
                                    s.copy(out=xqfm[:, d * T + G * D:d * T + (G + 1) * D],
                                           in_=ps_w2[:, d * D:(d + 1) * D]
                                           ).then_inc(x_ev, 1)

                if not _on("f2b"):
                    return nc
                with ExitStack() as pctx:
                    ps_mm1 = pctx.enter_context(
                        nc.psum_tensor("ps_mm1", [128, 4 * HALF], F32))
                    htb = pctx.enter_context(
                        nc.sbuf_tensor("htb", [128, 4 * HALF], F32))

                    with nc.Block() as blk:
                        @blk.tensor
                        def _(t):
                            for tt in range(4):
                                for fb in range(32):
                                    i = tt * 32 + fb
                                    if fb == 0:
                                        t.wait_ge(x_ev, 8 if tt < 2 else 16)
                                    if i >= 4:
                                        t.wait_ge(m_ev, i - 3)
                                    for d in range(8):
                                        ins = t.matmul(
                                            ps_mm1[:, (i % 4) * HALF:(i % 4 + 1) * HALF],
                                            w1qt[:, d * FF + fb * 128:
                                                 d * FF + fb * 128 + 128],
                                            xqfm[:, d * T + tt * 512:d * T + (tt + 1) * 512],
                                            start=(d == 0), stop=(d == 7))
                                    ins.then_inc(m_pe, 1)

                        @blk.scalar
                        def _(s):
                            for i in range(128):
                                fb = i % 32
                                s.wait_ge(m_pe, i + 1)
                                if i >= 4:
                                    s.wait_ge(m_vr, i - 3)
                                    s.wait_ge(msl[i % 4], 16 * (i // 4))
                                s.activation(out=htb[:, (i % 4) * HALF:(i % 4 + 1) * HALF],
                                             in_=ps_mm1[:, (i % 4) * HALF:(i % 4 + 1) * HALF],
                                             func=AF.Relu, bias=b1i[:, fb:fb + 1],
                                             scale=1.0).then_inc(m_ev, 1)

                        @blk.vector
                        def _(v):
                            for i in range(128):
                                tt, fb = divmod(i, 32)
                                v.wait_ge(m_ev, i + 1)
                                v.reduce_max(hrm4[:, fb * 4 + tt:fb * 4 + tt + 1],
                                             htb[:, (i % 4) * HALF:(i % 4 + 1) * HALF],
                                             axis=AX.X).then_inc(m_vr, 1)

                        @blk.gpsimd
                        def _(g):
                            for i in range(128):
                                tt, fb = divmod(i, 32)
                                g.wait_ge(m_ev, i + 1)
                                g.dma_start(
                                    out=h_d.ap()[tt, fb * 128:(fb + 1) * 128, :],
                                    in_=htb[:, (i % 4) * HALF:(i % 4 + 1) * HALF]
                                ).then_inc(msl[i % 4], 16)
                            for k in range(4):
                                g.wait_ge(msl[k], 16 * 32)

        h_vx = sem("h_vx"); h_lg = sem("h_lg"); h_cc = sem("h_cc")
        v_q = sem("v_q"); tp2 = sem("tp2"); te2 = sem("te2")
        w2l = [sem(f"w2l{k}") for k in range(4)]
        q_ld = sem("q_ld"); q_v = sem("q_v"); y_pe = sem("y_pe")
        y_ev = sem("y_ev"); yt_pe = sem("yt_pe"); yt_fr = sem("yt_fr")
        o_ld = sem("o_ld"); o_v = sem("o_v"); o_sq2 = sem("o_sq2")
        o_rec = sem("o_rec"); o_rsq = sem("o_rsq"); o_vv = sem("o_vv")
        o_st = sem("o_st"); dbg_q = sem("dbg_q"); dbg_y = sem("dbg_y")
        qh = [sem(f"qh{k}") for k in range(4)]
        osl = [sem(f"osl{k}") for k in range(4)]

        if not _on("g1"):
            return nc
        with ExitStack() as scopeG:
            SG = lambda n, s, d: scopeG.enter_context(nc.sbuf_tensor(n, s, d))
            w2qt = SG("w2qt", [128, 32 * D], BF16)
            hstg = SG("hstg", [128, 3 * HALF], F32)
            hqq = SG("hqq", [128, 2 * 32 * HALF], BF16)

            with ExitStack() as scopeG1:
                SG1 = lambda n, s, d: scopeG1.enter_context(nc.sbuf_tensor(n, s, d))
                w2c = SG1("w2c", [128, 2 * FF], F32)
                w2i = SG1("w2i", [128, 2048], I32)
                w2b = SG1("w2b", [128, 2 * FF], BF16)
                ps_w3 = scopeG1.enter_context(
                    nc.psum_tensor("ps_w3", [128, 2 * 8 * 128], BF16))

                with nc.Block() as blk:
                    @blk.sync
                    def _(s):
                        for j in range(16):
                            dc, h = divmod(j, 2)
                            if j == 0:
                                s.wait_ge(m_pe, 128)
                            elif j >= 4:
                                s.wait_ge(v_q, j - 3)
                            s.dma_start(
                                out=w2c[:, (j % 4) * 2048:(j % 4 + 1) * 2048],
                                in_=w2_d.ap()[dc * 128:(dc + 1) * 128,
                                              h * 2048:(h + 1) * 2048]
                            ).then_inc(w2l[j % 4], 16)

                    @blk.vector
                    def _(v):
                        v.wait_ge(m_vr, 128)
                        v.drain()
                        for fb in range(32):
                            v.reduce_max(hrm[:, fb:fb + 1],
                                         hrm4[:, fb * 4:(fb + 1) * 4], axis=AX.X)
                        v.drain()
                        v.tensor_tensor(hsc[:], hrm[:], s1v[:], op=ALU.mult)
                        v.drain()
                        v.reduce_max(hm1[:], hsc[:], axis=AX.X)
                        v.drain()
                        v.nop().then_inc(h_vx, 1)
                        v.wait_ge(h_lg, 32)
                        v.reduce_max(redsc[:], redrow[:], axis=AX.X)
                        v.drain()
                        v.nop().then_inc(h_vx, 1)

                        def w2red(dc, h):
                            j = dc * 2 + h
                            v.wait_ge(w2l[j % 4], 16 * (j // 4 + 1))
                            v.tensor_reduce(am2h[:, (dc % 2) * 2 + h:(dc % 2) * 2 + h + 1],
                                            w2c[:, (j % 4) * 2048:(j % 4 + 1) * 2048],
                                            axis=AX.X, op=ALU.max,
                                            apply_absolute_value=True)
                        w2red(0, 0)
                        w2red(0, 1)
                        v.drain()
                        for dc in range(8):
                            c2 = (dc % 2) * 2
                            if dc + 1 < 8:
                                w2red(dc + 1, 0)
                            v.tensor_tensor(am2[:, dc:dc + 1], am2h[:, c2:c2 + 1],
                                            am2h[:, c2 + 1:c2 + 2], op=ALU.max)
                            v.drain()
                            v.tensor_scalar(am2[:, dc:dc + 1], am2[:, dc:dc + 1],
                                            EPS_Q, None, op0=ALU.max)
                            v.drain()
                            v.reciprocal(rc2[:, dc:dc + 1], am2[:, dc:dc + 1])
                            v.drain()
                            v.tensor_scalar(w2i[:], w2c[:, (dc * 2 % 4) * 2048:
                                                        (dc * 2 % 4 + 1) * 2048],
                                            rc2[:, dc:dc + 1], 127.0,
                                            op0=ALU.mult, op1=ALU.mult)
                            if dc >= 2:
                                v.wait_ge(tp2, (dc - 1) * 4)
                            v.tensor_copy(w2b[:, (dc % 2) * FF:(dc % 2) * FF + 2048],
                                          w2i[:]).then_inc(v_q, 1)
                            if dc + 1 < 8:
                                w2red(dc + 1, 1)
                            else:
                                v.drain()
                            v.tensor_scalar(w2i[:], w2c[:, ((dc * 2 + 1) % 4) * 2048:
                                                        ((dc * 2 + 1) % 4 + 1) * 2048],
                                            rc2[:, dc:dc + 1], 127.0,
                                            op0=ALU.mult, op1=ALU.mult)
                            v.tensor_copy(w2b[:, (dc % 2) * FF + 2048:(dc % 2 + 1) * FF],
                                          w2i[:]).then_inc(v_q, 1)
                        v.wait_ge(h_lg, 64)
                        v.tensor_scalar(gmax2[:], gmax2[:], EPS_Q, None, op0=ALU.max)
                        v.drain()
                        v.tensor_scalar(shsc[:], gmax2[:], 1.0 / 255.0, None,
                                        op0=ALU.mult)
                        v.drain()
                        v.reciprocal(hrec[:], shsc[:])
                        v.drain()
                        v.tensor_scalar(qv2[:], s1v[:], hrec[:, 0:1], None, op0=ALU.mult)
                        v.tensor_scalar(s2v[:], am2[:], 1.0 / 127.0, None, op0=ALU.mult)
                        v.drain()
                        v.tensor_scalar(s2v[:], s2v[:], shsc[:, 0:1], None, op0=ALU.mult)
                        v.drain()
                        v.reciprocal(lnsc[:, 0:8], s2v[:])
                        v.drain()
                        v.tensor_tensor(hsc[:, 0:8], b2t[:], lnsc[:, 0:8],
                                        op=ALU.mult)
                        v.drain()
                        v.tensor_copy(ivec[:, 0:8], hsc[:, 0:8])
                        v.drain()
                        v.tensor_copy(b2i[:], ivec[:, 0:8])
                        v.drain()
                        v.nop().then_inc(h_vx, 1)

                    @blk.gpsimd
                    def _(g):
                        g.wait_ge(h_vx, 1)
                        g.dma_start(out=pm_d[:], in_=hm1[:, 0]).then_inc(h_lg, 16)
                        g.wait_ge(h_lg, 16)
                        g.dma_start(out=redrow[:], in_=pm_d[None, :]).then_inc(h_lg, 16)
                        g.wait_ge(h_vx, 2)
                        g.dma_start(out=cc2_in[:], in_=redsc[0, :]).then_inc(h_lg, 16)
                        g.wait_ge(h_lg, 48)
                        g.collective_compute(
                            "AllReduce", ALU.max,
                            replica_groups=[list(range(8))],
                            ins=[cc2_in.ap()], outs=[cc2_out.ap()],
                        ).then_inc(h_cc, 1)
                        g.wait_ge(h_cc, 1)
                        g.dma_start(out=gmax2[:],
                                    in_=cc2_out.ap()[None, :].broadcast_to((128, 1))
                                    ).then_inc(h_lg, 16)

                    @blk.tensor
                    def _(t):
                        for dc in range(8):
                            t.wait_ge(v_q, 2 * (dc + 1))
                            for g8 in range(4):
                                sg = dc * 4 + g8
                                if sg >= 2:
                                    t.wait_ge(te2, (sg - 1) * 8)
                                for fs_r in range(8):
                                    fs = g8 * 8 + fs_r
                                    ins = t.transpose(
                                        ps_w3[:, (sg % 2) * 1024 + fs_r * 128:
                                              (sg % 2) * 1024 + (fs_r + 1) * 128],
                                        w2b[:, (dc % 2) * FF + fs * 128:
                                            (dc % 2) * FF + fs * 128 + 128],
                                        idb[:])
                                ins.then_inc(tp2, 1)

                    @blk.scalar
                    def _(s):
                        for dc in range(8):
                            for g8 in range(4):
                                sg = dc * 4 + g8
                                s.wait_ge(tp2, sg + 1)
                                for fs_r in range(8):
                                    fs = g8 * 8 + fs_r
                                    s.copy(out=w2qt[:, fs * D + dc * 128:
                                                    fs * D + (dc + 1) * 128],
                                           in_=ps_w3[:, (sg % 2) * 1024 + fs_r * 128:
                                                     (sg % 2) * 1024 + (fs_r + 1) * 128]
                                           ).then_inc(te2, 1)

            if not _on("g2"):
                return nc
            with ExitStack() as scopeG2:
                SG2 = lambda n, s, d: scopeG2.enter_context(nc.sbuf_tensor(n, s, d))
                hqi = SG2("hqi", [128, 2 * HALF], I32)
                yf = SG2("yf", [128, 8 * HALF], F32)
                r1m = SG2("r1m", [128, 2 * D], F32)
                r1nm = SG2("r1nm", [128, D], F32)
                r2b = SG2("r2b", [128, 4 * D], F32)
                ps_y = scopeG2.enter_context(
                    nc.psum_tensor("ps_y", [128, 4 * HALF], F32))
                ps_yt = scopeG2.enter_context(
                    nc.psum_tensor("ps_yt", [128, 2 * D], F32))

                with nc.Block() as blk:
                    @blk.sync
                    def _(s):
                        for k in range(4):
                            s.wait_ge(msl[k], 16 * 32)
                        for idx in range(128):
                            tt, fb = divmod(idx, 32)
                            if idx >= 3:
                                s.wait_ge(q_v, idx - 2)
                            s.dma_start(
                                out=hstg[:, (idx % 3) * HALF:(idx % 3 + 1) * HALF],
                                in_=h_d.ap()[tt, fb * 128:(fb + 1) * 128, :]
                            ).then_inc(qh[idx % 3], 16)
                        s.wait_ge(o_st, 1)

                    @blk.gpsimd
                    def _(g):
                        for t_ in range(NT):
                            g.wait_ge(o_vv, t_ + 1)
                            g.dma_start(out=out_d.ap()[t_ * 128:(t_ + 1) * 128, :],
                                        in_=r2b[:, (t_ % 4) * D:(t_ % 4 + 1) * D]
                                        ).then_inc(osl[t_ % 4], 16)
                        for k in range(4):
                            g.wait_ge(osl[k], 16 * 4)
                        g.nop().then_inc(o_st, 1)

                    @blk.vector
                    def _(v):
                        def quant(tt):
                            if tt >= 2:
                                v.wait_ge(y_pe, (tt - 1) * 8)
                                if DEBUG:
                                    v.wait_ge(dbg_q, 16 * (tt - 1))
                            hq0 = (tt % 2) * 32 * HALF
                            for fb in range(32):
                                idx = tt * 32 + fb
                                q = (fb % 2) * HALF
                                v.wait_ge(qh[idx % 3], 16 * (idx // 3 + 1))
                                v.tensor_scalar(hqi[:, q:q + HALF],
                                                hstg[:, (idx % 3) * HALF:(idx % 3 + 1) * HALF],
                                                qv2[:, fb:fb + 1], None, op0=ALU.mult)
                                v.tensor_copy(hqq[:, hq0 + fb * 512:hq0 + (fb + 1) * 512],
                                              hqi[:, q:q + HALF]).then_inc(q_v, 1)

                        def posttt(tt):
                            for db in range(8):
                                i = tt * 8 + db
                                v.wait_ge(y_pe, i + 1)
                                if DEBUG and tt >= 1 and db == 0:
                                    v.wait_ge(dbg_y, 16 * tt)
                                v.tensor_scalar(yf[:, db * HALF:(db + 1) * HALF],
                                                ps_y[:, (i % 4) * HALF:(i % 4 + 1) * HALF],
                                                b2i[:, db:db + 1], s2v[:, db:db + 1],
                                                op0=ALU.add, op1=ALU.mult).then_inc(y_ev, 1)
                            for ts4 in range(4):
                                t_ = tt * 4 + ts4
                                v.wait_ge(yt_pe, t_ + 1)
                                v.wait_ge(o_ld, 16 * min(NT, t_ + 2))
                                if t_ >= 4:
                                    v.wait_ge(osl[t_ % 4], 16 * (t_ // 4))
                                src = r1m[:, (t_ % 2) * D:(t_ % 2 + 1) * D]
                                v.tensor_scalar(r1nm[:], src, nmean[:, t_:t_ + 1],
                                                rstd[:, t_:t_ + 1],
                                                op0=ALU.add, op1=ALU.mult)
                                v.tensor_tensor(r2b[:, ts4 * D:(ts4 + 1) * D],
                                                ps_yt[:, (t_ % 2) * D:(t_ % 2 + 1) * D],
                                                r1nm[:], op=ALU.add)
                                v.drain()
                                v.nop().then_inc(yt_fr, 1)
                                v.nop().then_inc(o_v, 1)
                                v.reduce_sum(statb[:, t_:t_ + 1],
                                             r2b[:, ts4 * D:(ts4 + 1) * D], axis=AX.X)
                            v.wait_ge(o_sq2, (tt + 1) * 4)
                            v.drain()
                            c0, c1 = tt * 4, (tt + 1) * 4
                            v.tensor_scalar(nmeanb[:, c0:c1], statb[:, c0:c1],
                                            -1.0 / D, None, op0=ALU.mult)
                            v.tensor_scalar(stat2b[:, c0:c1], stat2b[:, c0:c1],
                                            1.0 / D, None, op0=ALU.mult)
                            v.drain()
                            v.tensor_tensor(lnsc[:, c0:c1], nmeanb[:, c0:c1],
                                            nmeanb[:, c0:c1], op=ALU.mult)
                            v.drain()
                            v.tensor_tensor(stat2b[:, c0:c1], stat2b[:, c0:c1],
                                            lnsc[:, c0:c1], op=ALU.subtract)
                            v.drain()
                            v.tensor_scalar(stat2b[:, c0:c1], stat2b[:, c0:c1],
                                            EPS_LN, None, op0=ALU.add)
                            v.drain()
                            v.reciprocal(stat2b[:, c0:c1], stat2b[:, c0:c1])
                            v.drain()
                            v.nop().then_inc(o_rec, 1)
                            v.wait_ge(o_rsq, tt + 1)
                            for ts4 in range(4):
                                t_ = tt * 4 + ts4
                                v.tensor_scalar(r2b[:, ts4 * D:(ts4 + 1) * D],
                                                r2b[:, ts4 * D:(ts4 + 1) * D],
                                                nmeanb[:, t_:t_ + 1],
                                                rstdb[:, t_:t_ + 1],
                                                op0=ALU.add, op1=ALU.mult
                                                ).then_inc(o_vv, 1)

                        quant(0)
                        for tt in range(1, 4):
                            quant(tt)
                            posttt(tt - 1)
                        posttt(3)

                    @blk.scalar
                    def _(s):
                        s.dma_start(out=r1m[:, 0:D],
                                    in_=r1_d.ap()[0:128, :]).then_inc(o_ld, 16)
                        s.dma_start(out=r1m[:, D:2 * D],
                                    in_=r1_d.ap()[128:256, :]).then_inc(o_ld, 16)
                        for tt in range(4):
                            if DEBUG:
                                s.wait_ge(q_v, (tt + 1) * 32)
                                s.dma_start(out=hqq_dbg[tt],
                                            in_=hqq[:, (tt % 2) * 32 * HALF:
                                                    (tt % 2 + 1) * 32 * HALF]
                                            ).then_inc(dbg_q, 16)
                                s.wait_ge(y_ev, (tt + 1) * 8)
                                s.dma_start(out=yf_dbg[tt], in_=yf[:]
                                            ).then_inc(dbg_y, 16)
                            for ts4 in range(4):
                                t_ = tt * 4 + ts4
                                s.wait_ge(o_v, t_ + 1)
                                s.activation(out=sq_scr[:],
                                             in_=r2b[:, ts4 * D:(ts4 + 1) * D],
                                             func=AF.Square,
                                             accum_out=stat2b[:, t_:t_ + 1]
                                             ).then_inc(o_sq2, 1)
                                if t_ + 2 < NT:
                                    tn = t_ + 2
                                    s.dma_start(out=r1m[:, (tn % 2) * D:(tn % 2 + 1) * D],
                                                in_=r1_d.ap()[tn * 128:(tn + 1) * 128, :]
                                                ).then_inc(o_ld, 16)
                            s.wait_ge(o_rec, tt + 1)
                            s.sqrt(rstdb[:, tt * 4:(tt + 1) * 4],
                                   stat2b[:, tt * 4:(tt + 1) * 4]).then_inc(o_rsq, 1)

                    @blk.tensor
                    def _(t):
                        for tt in range(4):
                            t.wait_ge(q_v, (tt + 1) * 32)
                            hq0 = (tt % 2) * 32 * HALF
                            for db in range(8):
                                i = tt * 8 + db
                                if i >= 4:
                                    t.wait_ge(y_ev, i - 3)
                                for fs in range(32):
                                    ins = t.matmul(
                                        ps_y[:, (i % 4) * HALF:(i % 4 + 1) * HALF],
                                        w2qt[:, fs * D + db * 128:fs * D + db * 128 + 128],
                                        hqq[:, hq0 + fs * HALF:hq0 + (fs + 1) * HALF],
                                        start=(fs == 0), stop=(fs == 31))
                                ins.then_inc(y_pe, 1)
                            t.wait_ge(y_ev, (tt + 1) * 8)
                            for ts4 in range(4):
                                t_ = tt * 4 + ts4
                                if t_ >= 2:
                                    t.wait_ge(yt_fr, t_ - 1)
                                for db in range(8):
                                    ins = t.transpose(
                                        ps_yt[:, (t_ % 2) * D + db * 128:
                                              (t_ % 2) * D + db * 128 + 128],
                                        yf[:, db * HALF + ts4 * 128:
                                           db * HALF + (ts4 + 1) * 128],
                                        idf[:])
                                ins.then_inc(yt_pe, 1)

        if DEBUG:
            d_st = sem("d_st")
            with nc.Block() as blk:
                @blk.sync
                def _(s):
                    s.dma_start(out=st_dbg[:, 0:16], in_=stat[:]).then_inc(d_st, 16)
                    s.dma_start(out=st_dbg[:, 16:32], in_=nmean[:]).then_inc(d_st, 16)
                    s.dma_start(out=st_dbg[:, 32:48], in_=rstd[:]).then_inc(d_st, 16)
                    s.dma_start(out=st_dbg[:, 48:64], in_=xmaxr[:]).then_inc(d_st, 16)
                    s.wait_ge(d_st, 64)

    return nc


def _host_tables(seg_start):
    j = np.arange(HALF, dtype=np.float64)
    scale_vec = (2.0 * j + 0.4 * D) / (1.4 * D)
    inv_freq = 1.0 / 10000.0 ** (j / HALF)
    c = seg_start + 512.0
    lg = np.log(np.float64(GAMMA))

    l_own = np.arange(T, dtype=np.float64)
    p_own = seg_start + l_own
    theta = inv_freq[:, None] * p_own[None, :]
    aq = scale_vec[:, None] ** ((p_own[None, :] - c) / 512.0)
    gq = np.exp((l_own - 1024.0) * lg)
    ak = scale_vec[:, None] ** (-(p_own[None, :] - c) / 512.0)
    gk = np.exp((1024.0 - l_own) * lg)
    cq = (np.cos(theta) * aq * gq[None, :]).astype(np.float32)
    sq = (np.sin(theta) * aq * gq[None, :]).astype(np.float32)
    ck = (np.cos(theta) * ak * gk[None, :]).astype(np.float32)
    sk = (np.sin(theta) * ak * gk[None, :]).astype(np.float32)

    lw = np.arange(-WRM, 0, dtype=np.float64)
    pw = seg_start + lw
    thw = pw[:, None] * inv_freq[None, :]
    akw = scale_vec[None, :] ** (-(pw[:, None] - c) / 512.0)
    gkw = np.exp((1024.0 - lw) * lg)
    ckw = (np.cos(thw) * akw * gkw[:, None]).astype(np.float32)
    skw = (np.sin(thw) * akw * gkw[:, None]).astype(np.float32)
    return cq, sq, ck, sk, ckw, skw


def prep_in_maps(L0_attn_X, L0_attn_W_Q, L0_attn_W_K, L0_attn_W_V,
                 L0_ffn_w1, L0_ffn_b1, L0_ffn_w2, L0_ffn_b2):
    X = np.asarray(L0_attn_X, dtype=np.float32)
    perm = np.concatenate([np.arange(0, D, 2), np.arange(1, D, 2)])
    wq = np.ascontiguousarray(np.asarray(L0_attn_W_Q, np.float32)[:, perm])
    wk = np.ascontiguousarray(np.asarray(L0_attn_W_K, np.float32)[:, perm])
    wv = np.ascontiguousarray(np.asarray(L0_attn_W_V, np.float32))
    w1 = np.ascontiguousarray(np.asarray(L0_ffn_w1, np.float32))
    w2 = np.ascontiguousarray(np.asarray(L0_ffn_w2, np.float32))
    b1t = np.ascontiguousarray(np.asarray(L0_ffn_b1, np.float32).reshape(32, 128).T)
    b2t = np.ascontiguousarray(np.asarray(L0_ffn_b2, np.float32).reshape(8, 128).T)
    tri = np.triu(np.ones((128, 128))).astype(ml_dtypes.bfloat16)
    idb = np.eye(128).astype(ml_dtypes.bfloat16)
    idf = np.eye(128, dtype=np.float32)

    tables = [_host_tables(seg * T) for seg in range(4)]
    zeros_warm = np.zeros((WRM, D), np.float32)

    in_maps = []
    for core in range(8):
        seq, seg = divmod(core, 4)
        cq, sq, ck, sk, ckw, skw = tables[seg]
        xo = np.ascontiguousarray(X[seq, seg * T:(seg + 1) * T, :])
        xw = (zeros_warm if seg == 0 else
              np.ascontiguousarray(X[seq, seg * T - WRM:seg * T, :]))
        in_maps.append(dict(
            x_own=xo, x_warm=xw, wq=wq, wk=wk, wv=wv, w1=w1, w2=w2,
            b1t=b1t, b2t=b2t, cq=cq, sq=sq, ck=ck, sk=sk, ckw=ckw, skw=skw,
            tri=tri, idb=idb, idf=idf))
    return in_maps


def kernel(L0_attn_X, L0_attn_W_Q, L0_attn_W_K, L0_attn_W_V,
           L0_ffn_w1, L0_ffn_b1, L0_ffn_w2, L0_ffn_b2):
    in_maps = prep_in_maps(L0_attn_X, L0_attn_W_Q, L0_attn_W_K, L0_attn_W_V,
                           L0_ffn_w1, L0_ffn_b1, L0_ffn_w2, L0_ffn_b2)
    nc = build()
    res = run_bass_kernel_spmd(nc, in_maps, core_ids=list(range(8)))
    out = np.empty((2, 8192, D), np.float32)
    for core in range(8):
        seq, seg = divmod(core, 4)
        out[seq, seg * T:(seg + 1) * T, :] = res.results[core]["out"]
    return out


# version 12 (final)
# speedup vs baseline: 1.1125x; 1.0447x over previous
import numpy as np
import ml_dtypes
from contextlib import ExitStack

import concourse.bass as bass
import concourse.mybir as mybir
from concourse.bass_utils import run_bass_kernel_spmd

F32 = mybir.dt.float32
BF16 = mybir.dt.bfloat16
I32 = mybir.dt.int32
AF = mybir.ActivationFunctionType
ALU = mybir.AluOpType
AX = mybir.AxisListType

D = 1024
HALF = 512
FF = 4096
T = 2048
WRM = 512
NW = WRM // 128
NT = T // 128
NG = 4
GAMMA = 0.96875
EPS_LN = 1e-5
EPS_Q = 1e-8
DEBUG = False
UPTO = "full"
_STAGES = ["warm", "attn", "f1", "f2a", "f2b", "g1", "g2"]


def _on(stage):
    return _STAGES.index(stage) <= _STAGES.index(UPTO) if UPTO != "full" else True


def pes_before(n):
    return n * (n + 1) // 2




def dma3(eng, out2d, in2d, n0, n1):
    return eng.dma_start(
        out=out2d.rearrange("p (a b) -> p a b", a=n0, b=n1),
        in_=in2d.rearrange("(a p) b -> p a b", p=128))

def build():
    nc = bass.Bass("TRN2", target_bir_lowering=False,
                   detect_race_conditions=False)

    x_own = nc.dram_tensor("x_own", [T, D], F32, kind="ExternalInput")
    x_warm = nc.dram_tensor("x_warm", [WRM, D], F32, kind="ExternalInput")
    wq_d = nc.dram_tensor("wq", [D, D], F32, kind="ExternalInput")
    wk_d = nc.dram_tensor("wk", [D, D], F32, kind="ExternalInput")
    wv_d = nc.dram_tensor("wv", [D, D], F32, kind="ExternalInput")
    w1_d = nc.dram_tensor("w1", [FF, D], F32, kind="ExternalInput")
    w2_d = nc.dram_tensor("w2", [D, FF], F32, kind="ExternalInput")
    b1_d = nc.dram_tensor("b1t", [128, 32], F32, kind="ExternalInput")
    b2_d = nc.dram_tensor("b2t", [128, 8], F32, kind="ExternalInput")
    cq_d = nc.dram_tensor("cq", [HALF, T], F32, kind="ExternalInput")
    sq_d = nc.dram_tensor("sq", [HALF, T], F32, kind="ExternalInput")
    ck_d = nc.dram_tensor("ck", [HALF, T], F32, kind="ExternalInput")
    sk_d = nc.dram_tensor("sk", [HALF, T], F32, kind="ExternalInput")
    ckw_d = nc.dram_tensor("ckw", [WRM, HALF], F32, kind="ExternalInput")
    skw_d = nc.dram_tensor("skw", [WRM, HALF], F32, kind="ExternalInput")
    tri_d = nc.dram_tensor("tri", [128, 128], BF16, kind="ExternalInput")
    idb_d = nc.dram_tensor("idb", [128, 128], BF16, kind="ExternalInput")
    idf_d = nc.dram_tensor("idf", [128, 128], F32, kind="ExternalInput")
    out_d = nc.dram_tensor("out", [T, D], F32, kind="ExternalOutput")

    dbg = "ExternalOutput" if DEBUG else "Internal"
    r1_d = nc.dram_tensor("r1_spill", [T, D], F32, kind=dbg)
    h_d = nc.dram_tensor("h_spill", [4, FF, 512], F32, kind=dbg)
    pm_d = nc.dram_tensor("pm_bounce", [128], F32)
    cc1_in = nc.dram_tensor("cc1_in", [1], F32)
    cc1_out = nc.dram_tensor("cc1_out", [1], F32)
    cc2_in = nc.dram_tensor("cc2_in", [1], F32)
    cc2_out = nc.dram_tensor("cc2_out", [1], F32)
    if DEBUG:
        kx_dbg = nc.dram_tensor("kx_dbg", [128, 8 * T], BF16, kind="ExternalOutput")
        vt_dbg = nc.dram_tensor("vt_dbg", [128, NT * D], BF16, kind="ExternalOutput")
        sin_dbg = nc.dram_tensor("sin_dbg", [128, 8 * D], BF16, kind="ExternalOutput")
        st_dbg = nc.dram_tensor("st_dbg", [128, 64], F32, kind="ExternalOutput")
        hqq_dbg = nc.dram_tensor("hqq_dbg", [4, 128, 32 * HALF], BF16,
                                 kind="ExternalOutput")
        yf_dbg = nc.dram_tensor("yf_dbg", [4, 128, 8 * HALF], F32,
                                kind="ExternalOutput")

    with ExitStack() as top:
        SB = lambda n, s, d: top.enter_context(nc.sbuf_tensor(n, s, d))
        sem = lambda n: top.enter_context(nc.semaphore(n))

        tri = SB("tri_sb", [128, 128], BF16)
        idb = SB("idb_sb", [128, 128], BF16)
        idf = SB("idf_sb", [128, 128], F32)
        b1t = SB("b1t_sb", [128, 32], F32)
        b2t = SB("b2t_sb", [128, 8], F32)
        stat = SB("stat_sb", [128, 16], F32)
        stat2 = SB("stat2_sb", [128, 16], F32)
        nmean = SB("nmean_sb", [128, 16], F32)
        rstd = SB("rstd_sb", [128, 16], F32)
        xmaxr = SB("xmaxr_sb", [128, 16], F32)
        statb = SB("statb_sb", [128, 16], F32)
        stat2b = SB("stat2b_sb", [128, 16], F32)
        nmeanb = SB("nmeanb_sb", [128, 16], F32)
        rstdb = SB("rstdb_sb", [128, 16], F32)
        sq_scr = SB("sq_scr", [128, D], F32)
        redrow = SB("redrow", [1, 128], F32)
        redsc = SB("redsc", [1, 1], F32)
        gmax1 = SB("gmax1", [128, 1], F32)
        sinsc = SB("sinsc", [128, 1], F32)
        rinv1 = SB("rinv1", [128, 1], F32)
        gmax2 = SB("gmax2", [128, 1], F32)
        shsc = SB("shsc", [128, 1], F32)
        hrec = SB("hrec", [128, 1], F32)
        s1v = SB("s1v", [128, 32], F32)
        qv2 = SB("qv2", [128, 32], F32)
        s2v = SB("s2v", [128, 8], F32)
        b1i = SB("b1i", [128, 32], F32)
        b2i = SB("b2i", [128, 8], F32)
        am1 = SB("am1", [128, 32], F32)
        am2 = SB("am2", [128, 8], F32)
        am2h = SB("am2h", [128, 4], F32)
        rc1 = SB("rc1", [128, 32], F32)
        rc2 = SB("rc2", [128, 8], F32)
        hrm4 = SB("hrm4", [128, 128], F32)
        hrm = SB("hrm", [128, 32], F32)
        hsc = SB("hsc", [128, 32], F32)
        lnsc = SB("lnsc", [128, 16], F32)
        xm1 = SB("xm1", [128, 1], F32)
        hm1 = SB("hm1", [128, 1], F32)
        ivec = SB("ivec", [128, 32], I32)

        with ExitStack() as scopeA:
            SA = lambda n, s, d: scopeA.enter_context(nc.sbuf_tensor(n, s, d))
            wqkv = SA("wqkv", [128, 3 * 8 * D], BF16)
            sin_sb = SA("sin_sb", [128, 8 * D], BF16)

            w_ld = sem("w_ld"); w_lg = sem("w_lg"); w_pet = sem("w_pet")
            w_evt = sem("w_evt"); w_pek = sem("w_pek"); w_evk = sem("w_evk")
            w_vx = sem("w_vx"); w_pes = sem("w_pes"); w_evs = sem("w_evs")

            with ExitStack() as wctx:
                WSB = lambda n, s, d: wctx.enter_context(nc.sbuf_tensor(n, s, d))
                xwbf = WSB("xwbf", [128, NW * D], BF16)
                xwfm = WSB("xwfm", [128, 8 * WRM], BF16)
                kxw = WSB("kxw", [128, NW * D], BF16)
                vw = WSB("vw", [128, NW * D], BF16)
                kwraw = WSB("kwraw", [128, 2 * D], BF16)
                twbl = WSB("twbl", [128, 2 * D], F32)
                xs1 = WSB("xs1", [128, HALF], F32)
                xs2 = WSB("xs2", [128, HALF], F32)

                with ExitStack() as pctx:
                    ps_bf = pctx.enter_context(
                        nc.psum_tensor("ps_bf", [128, 2 * 1024], BF16))
                    ps_f = pctx.enter_context(
                        nc.psum_tensor("ps_f", [128, 4 * HALF], F32))

                    with nc.Block() as blk:
                        @blk.sync
                        def _(s):
                            for t_, d_ in ((tri, tri_d), (idb, idb_d), (idf, idf_d),
                                           (b1t, b1_d), (b2t, b2_d)):
                                s.dma_start(out=t_[:], in_=d_[:]).then_inc(w_ld, 16)
                            for m in range(NW):
                                p = m % 2
                                if m >= 2:
                                    s.wait_ge(w_vx, m - 1)
                                s.dma_start(out=twbl[:, p * D:p * D + HALF],
                                            in_=ckw_d[m * 128:(m + 1) * 128, :]
                                            ).then_inc(w_ld, 16)
                                s.dma_start(out=twbl[:, p * D + HALF:(p + 1) * D],
                                            in_=skw_d[m * 128:(m + 1) * 128, :]
                                            ).then_inc(w_ld, 16)

                        @blk.gpsimd
                        def _(g):
                            dma3(g, xwbf[:], x_warm.ap(), NW, D).then_inc(w_lg, 16)
                            for i, wd in enumerate((wq_d, wk_d, wv_d)):
                                dma3(g, wqkv[:, i * 8 * D:(i + 1) * 8 * D],
                                     wd.ap(), 8, D).then_inc(w_lg, 16)

                        @blk.tensor
                        def _(t):
                            t.wait_ge(w_lg, 64)
                            t.wait_ge(w_ld, 144)
                            for d in range(8):
                                p = d % 2
                                if d >= 2:
                                    t.wait_ge(w_evt, d - 1)
                                for wt in range(NW):
                                    ins = t.transpose(
                                        ps_bf[:, p * 1024 + wt * 128:
                                              p * 1024 + (wt + 1) * 128],
                                        xwbf[:, wt * D + d * 128:wt * D + (d + 1) * 128],
                                        idb[:])
                                ins.then_inc(w_pet, 1)
                            t.wait_ge(w_evt, 8)
                            i = 0
                            for m in range(NW):
                                for which in (1, 2):
                                    for hv in range(2):
                                        slot = i % 4
                                        if i >= 4:
                                            t.wait_ge(w_evk, i - 3)
                                        for d in range(8):
                                            ins = t.matmul(
                                                ps_f[:, slot * HALF:(slot + 1) * HALF],
                                                xwfm[:, d * WRM + m * 128:
                                                     d * WRM + (m + 1) * 128],
                                                wqkv[:, which * 8 * D + d * D + hv * HALF:
                                                     which * 8 * D + d * D + hv * HALF + HALF],
                                                start=(d == 0), stop=(d == 7))
                                        ins.then_inc(w_pek, 1)
                                        i += 1

                        @blk.scalar
                        def _(s):
                            for d in range(8):
                                p = d % 2
                                s.wait_ge(w_pet, d + 1)
                                s.copy(out=xwfm[:, d * WRM:(d + 1) * WRM],
                                       in_=ps_bf[:, p * 1024:p * 1024 + WRM]
                                       ).then_inc(w_evt, 1)
                            i = 0
                            for m in range(NW):
                                for which in (1, 2):
                                    for hv in range(2):
                                        slot = i % 4
                                        s.wait_ge(w_pek, i + 1)
                                        if which == 1:
                                            if m >= 2:
                                                s.wait_ge(w_vx, m - 1)
                                            dst = kwraw[:, (m % 2) * D + hv * HALF:
                                                        (m % 2) * D + hv * HALF + HALF]
                                        else:
                                            dst = vw[:, m * D + hv * HALF:
                                                     m * D + hv * HALF + HALF]
                                        s.copy(out=dst,
                                               in_=ps_f[:, slot * HALF:(slot + 1) * HALF]
                                               ).then_inc(w_evk, 1)
                                        i += 1

                        @blk.vector
                        def _(v):
                            for m in range(NW):
                                p = m % 2
                                v.wait_ge(w_evk, 4 * m + 2)
                                v.wait_ge(w_ld, 80 + 32 * min(NW, m + 2))
                                cw = twbl[:, p * D:p * D + HALF]
                                sw = twbl[:, p * D + HALF:(p + 1) * D]
                                k1 = kwraw[:, (m % 2) * D:(m % 2) * D + HALF]
                                k2 = kwraw[:, (m % 2) * D + HALF:(m % 2) * D + D]
                                v.tensor_tensor(xs1[:], k1, cw, op=ALU.mult)
                                v.tensor_tensor(xs2[:], k2, sw, op=ALU.mult)
                                v.tensor_tensor(kxw[:, m * D:m * D + HALF],
                                                xs1[:], xs2[:], op=ALU.subtract)
                                v.tensor_tensor(xs1[:], k2, cw, op=ALU.mult)
                                v.tensor_tensor(xs2[:], k1, sw, op=ALU.mult)
                                v.tensor_tensor(kxw[:, m * D + HALF:(m + 1) * D],
                                                xs1[:], xs2[:], op=ALU.add
                                                ).then_inc(w_vx, 1)

                with ExitStack() as pctx:
                    ps_si = pctx.enter_context(
                        nc.psum_tensor("ps_si", [128, 8 * HALF], F32))

                    with nc.Block() as blk:
                        @blk.tensor
                        def _(t):
                            for pas in range(2):
                                if pas == 1:
                                    t.wait_ge(w_evs, 8)
                                for m in range(NW):
                                    for dk in range(4):
                                        for hv in range(2):
                                            ins = t.matmul(
                                                ps_si[:, (dk * 2 + hv) * HALF:
                                                      (dk * 2 + hv + 1) * HALF],
                                                kxw[:, m * D + (pas * 4 + dk) * 128:
                                                    m * D + (pas * 4 + dk) * 128 + 128],
                                                vw[:, m * D + hv * HALF:
                                                   m * D + hv * HALF + HALF],
                                                start=(m == 0), stop=(m == NW - 1))
                                ins.then_inc(w_pes, 1)

                        @blk.scalar
                        def _(s):
                            for pas in range(2):
                                s.wait_ge(w_pes, pas + 1)
                                for dk in range(4):
                                    for hv in range(2):
                                        s.copy(out=sin_sb[:, (pas * 4 + dk) * D + hv * HALF:
                                                          (pas * 4 + dk) * D + hv * HALF + HALF],
                                               in_=ps_si[:, (dk * 2 + hv) * HALF:
                                                         (dk * 2 + hv + 1) * HALF]
                                               ).then_inc(w_evs, 1)

            a_lg = sem("a_lg"); a_tx = sem("a_tx"); a_pet = sem("a_pet")
            a_evt = sem("a_evt"); a_tbl = sem("a_tbl"); a_vt = sem("a_vt")
            a_pea = sem("a_pea"); a_eva = sem("a_eva"); a_vxa = sem("a_vxa")
            a_pes = sem("a_pes"); a_evs = sem("a_evs"); a_peo = sem("a_peo")
            a_pfo = sem("a_pfo"); a_ldr = sem("a_ldr"); a_str = sem("a_str")
            a_sq = sem("a_sq")
            PEA_G = 24

            if not _on("attn"):
                return nc
            with ExitStack() as actx:
                ASB = lambda n, s, d: actx.enter_context(nc.sbuf_tensor(n, s, d))
                kx = ASB("kx", [128, 8 * T], BF16)
                vt = ASB("vt", [128, NT * D], BF16)
                xbf = ASB("xbf", [128, 4 * D], BF16)
                xfm = ASB("xfm", [128, 8 * HALF], BF16)
                qx = ASB("qx", [128, 8 * HALF], BF16)
                raw = ASB("raw", [128, 8 * HALF], BF16)
                tblc = ASB("tblc", [128, 4 * HALF], F32)
                tbls = ASB("tbls", [128, 4 * HALF], F32)
                scb = ASB("scb", [128, 2 * 128], BF16)
                xr = ASB("xr", [128, D], F32)
                r1b = ASB("r1b", [128, 2 * D], F32)
                ax1 = ASB("ax1", [128, HALF], F32)
                ax2 = ASB("ax2", [128, HALF], F32)
                ps_main = actx.enter_context(
                    nc.psum_tensor("ps_main", [128, 4 * HALF], F32))
                ps_sc = actx.enter_context(
                    nc.psum_tensor("ps_sc", [128, 2 * HALF], F32))
                ps_bf2 = actx.enter_context(
                    nc.psum_tensor("ps_bf2", [128, 2 * 1024], BF16))

                with nc.Block() as blk:
                    @blk.gpsimd
                    def _(g_):
                        for g in range(NG):
                            if g >= 1:
                                g_.wait_ge(a_tx, g)
                            sl = 0
                            src = x_own.ap()[g * 512:(g + 1) * 512, :]
                            dma3(g_, xbf[:, sl:sl + 4 * D], src, 4, D).then_inc(a_lg, 16)

                    @blk.sync
                    def _(s):
                        for g in range(NG):
                            if g >= 1:
                                s.wait_ge(a_vxa, g * 8)
                            for tb, dram in ((tblc, ck_d), (tbls, sk_d)):
                                dma3(s, tb[:], dram.ap()[:, g * HALF:(g + 1) * HALF],
                                     4, HALF).then_inc(a_tbl, 16)
                            s.wait_ge(a_vxa, g * 8 + 4)
                            for tb, dram in ((tblc, cq_d), (tbls, sq_d)):
                                dma3(s, tb[:], dram.ap()[:, g * HALF:(g + 1) * HALF],
                                     4, HALF).then_inc(a_tbl, 16)
                            for nn in range(4):
                                n = g * 4 + nn
                                if n >= 1:
                                    s.wait_ge(a_vt, n)
                                s.dma_start(out=xr[:],
                                            in_=x_own.ap()[n * 128:(n + 1) * 128, :]
                                            ).then_inc(a_ldr, 16)
                                s.wait_ge(a_vt, n + 1)
                                s.dma_start(out=r1_d.ap()[n * 128:(n + 1) * 128, :],
                                            in_=r1b[:, (n % 2) * D:(n % 2 + 1) * D]
                                            ).then_inc(a_str, 16)
                        s.wait_ge(a_str, 16 * NT)

                    @blk.tensor
                    def _(t):
                        for g in range(NG):
                            t.wait_ge(a_lg, (g + 1) * 16)
                            for d in range(8):
                                p = d % 2
                                if g * 8 + d >= 2:
                                    t.wait_ge(a_evt, g * 8 + d - 1)
                                xsl = 0
                                for wt in range(4):
                                    ins = t.transpose(
                                        ps_bf2[:, p * 1024 + wt * 128:
                                               p * 1024 + (wt + 1) * 128],
                                        xbf[:, xsl + wt * D + d * 128:
                                            xsl + wt * D + (d + 1) * 128],
                                        idb[:])
                                ins.then_inc(a_pet, 1)
                            t.nop().then_inc(a_tx, 1)
                            t.wait_ge(a_evt, (g + 1) * 8)
                            t.wait_ge(a_pfo, g * 4)
                            i0 = g * PEA_G
                            for fc in range(8):
                                i = i0 + fc
                                if i >= 4:
                                    t.wait_ge(a_eva, i - 3)
                                for d in range(8):
                                    ins = t.matmul(
                                        ps_main[:, (i % 4) * HALF:(i % 4 + 1) * HALF],
                                        wqkv[:, 8 * D + d * D + fc * 128:
                                             8 * D + d * D + fc * 128 + 128],
                                        xfm[:, d * HALF:(d + 1) * HALF],
                                        start=(d == 0), stop=(d == 7))
                                ins.then_inc(a_pea, 1)
                            for fc in range(8):
                                i = i0 + 8 + fc
                                t.wait_ge(a_eva, i - 3)
                                for d in range(8):
                                    ins = t.matmul(
                                        ps_main[:, (i % 4) * HALF:(i % 4 + 1) * HALF],
                                        wqkv[:, d * D + fc * 128:d * D + fc * 128 + 128],
                                        xfm[:, d * HALF:(d + 1) * HALF],
                                        start=(d == 0), stop=(d == 7))
                                ins.then_inc(a_pea, 1)
                            for tb in range(4):
                                for hv in range(2):
                                    i = i0 + 16 + tb * 2 + hv
                                    t.wait_ge(a_eva, i - 3)
                                    for d in range(8):
                                        ins = t.matmul(
                                            ps_main[:, (i % 4) * HALF:(i % 4 + 1) * HALF],
                                            xfm[:, d * HALF + tb * 128:
                                                d * HALF + (tb + 1) * 128],
                                            wqkv[:, 16 * D + d * D + hv * HALF:
                                                 16 * D + d * D + hv * HALF + HALF],
                                            start=(d == 0), stop=(d == 7))
                                    ins.then_inc(a_pea, 1)
                            t.wait_ge(a_vxa, g * 8 + 8)
                            t.wait_ge(a_eva, (g + 1) * PEA_G)
                            for nn in range(4):
                                n = g * 4 + nn
                                po = (n % 2) * D
                                if n >= 2:
                                    t.wait_ge(a_pfo, n - 1)
                                for d in range(8):
                                    for hv in range(2):
                                        t.matmul(
                                            ps_main[:, po + hv * HALF:po + hv * HALF + HALF],
                                            qx[:, d * HALF + nn * 128:
                                               d * HALF + nn * 128 + 128],
                                            sin_sb[:, d * D + hv * HALF:
                                                   d * D + hv * HALF + HALF],
                                            start=(d == 0), stop=False)
                                base_s = pes_before(n)
                                for m in range(n + 1):
                                    sl_s = m % 2
                                    if m >= 2:
                                        t.wait_ge(a_evs, base_s + m - 1)
                                    for d in range(8):
                                        ins = t.matmul(
                                            ps_sc[:, sl_s * HALF:sl_s * HALF + 128],
                                            kx[:, d * T + m * 128:d * T + m * 128 + 128],
                                            qx[:, d * HALF + nn * 128:
                                               d * HALF + nn * 128 + 128],
                                            start=(d == 0), stop=(d == 7))
                                    ins.then_inc(a_pes, 1)
                                    if m > 0:
                                        t.wait_ge(a_evs, base_s + m)
                                        for hv in range(2):
                                            t.matmul(
                                                ps_main[:, po + hv * HALF:
                                                        po + hv * HALF + HALF],
                                                scb[:, ((m - 1) % 2) * 128:
                                                    ((m - 1) % 2 + 1) * 128],
                                                vt[:, (m - 1) * D + hv * HALF:
                                                   (m - 1) * D + hv * HALF + HALF],
                                                start=False, stop=False)
                                t.wait_ge(a_evs, base_s + n + 1)
                                for hv in range(2):
                                    ins = t.matmul(
                                        ps_main[:, po + hv * HALF:po + hv * HALF + HALF],
                                        scb[:, (n % 2) * 128:(n % 2 + 1) * 128],
                                        vt[:, n * D + hv * HALF:n * D + hv * HALF + HALF],
                                        start=False, stop=True)
                                ins.then_inc(a_peo, 1)

                    @blk.scalar
                    def _(s):
                        for g in range(NG):
                            for d in range(8):
                                p = d % 2
                                s.wait_ge(a_pet, g * 8 + d + 1)
                                s.copy(out=xfm[:, d * HALF:(d + 1) * HALF],
                                       in_=ps_bf2[:, p * 1024:p * 1024 + HALF]
                                       ).then_inc(a_evt, 1)
                            i0 = g * PEA_G
                            for j in range(PEA_G):
                                i = i0 + j
                                s.wait_ge(a_pea, i + 1)
                                slot = i % 4
                                if j < 8:
                                    if j == 0:
                                        s.wait_ge(a_vxa, g * 8)
                                    dst = raw[:, j * HALF:(j + 1) * HALF]
                                elif j < 16:
                                    if j == 8:
                                        s.wait_ge(a_vxa, g * 8 + 4)
                                    dst = raw[:, (j - 8) * HALF:(j - 7) * HALF]
                                else:
                                    tb, hv = divmod(j - 16, 2)
                                    tg = g * 4 + tb
                                    dst = vt[:, tg * D + hv * HALF:tg * D + hv * HALF + HALF]
                                s.copy(out=dst,
                                       in_=ps_main[:, slot * HALF:(slot + 1) * HALF]
                                       ).then_inc(a_eva, 1)
                            for nn in range(4):
                                n = g * 4 + nn
                                s.wait_ge(a_vt, n + 1)
                                s.activation(out=sq_scr[:],
                                             in_=r1b[:, (n % 2) * D:(n % 2 + 1) * D],
                                             func=AF.Square,
                                             accum_out=stat2[:, n:n + 1]).then_inc(a_sq, 1)

                    @blk.vector
                    def _(v):
                        v.wait_ge(w_ld, 80)
                        for g in range(NG):
                            for which in range(2):
                                for j in range(4):
                                    v.wait_ge(a_eva, g * PEA_G + which * 8 + j + 5)
                                    v.wait_ge(a_tbl, g * 64 + 32 + which * 32)
                                    cw = tblc[:, j * HALF:(j + 1) * HALF]
                                    sw = tbls[:, j * HALF:(j + 1) * HALF]
                                    r1_ = raw[:, j * HALF:(j + 1) * HALF]
                                    r2_ = raw[:, (j + 4) * HALF:(j + 5) * HALF]
                                    if which == 0:
                                        o1 = kx[:, j * T + g * HALF:j * T + (g + 1) * HALF]
                                        o2 = kx[:, (j + 4) * T + g * HALF:
                                                (j + 4) * T + (g + 1) * HALF]
                                    else:
                                        o1 = qx[:, j * HALF:(j + 1) * HALF]
                                        o2 = qx[:, (j + 4) * HALF:(j + 5) * HALF]
                                    v.tensor_tensor(ax1[:], r1_, cw, op=ALU.mult)
                                    v.tensor_tensor(ax2[:], r2_, sw, op=ALU.mult)
                                    v.tensor_tensor(o1, ax1[:], ax2[:], op=ALU.subtract)
                                    v.tensor_tensor(ax1[:], r2_, cw, op=ALU.mult)
                                    v.tensor_tensor(ax2[:], r1_, sw, op=ALU.mult)
                                    v.tensor_tensor(o2, ax1[:], ax2[:], op=ALU.add
                                                    ).then_inc(a_vxa, 1)
                            for nn in range(4):
                                n = g * 4 + nn
                                base_s = pes_before(n)
                                for m in range(n + 1):
                                    v.wait_ge(a_pes, base_s + m + 1)
                                    sl_s = m % 2
                                    if m == n:
                                        v.tensor_tensor(
                                            scb[:, sl_s * 128:(sl_s + 1) * 128],
                                            ps_sc[:, sl_s * HALF:sl_s * HALF + 128],
                                            tri[:], op=ALU.mult).then_inc(a_evs, 1)
                                    else:
                                        v.tensor_copy(
                                            scb[:, sl_s * 128:(sl_s + 1) * 128],
                                            ps_sc[:, sl_s * HALF:sl_s * HALF + 128]
                                            ).then_inc(a_evs, 1)
                                po = (n % 2) * D
                                v.wait_ge(a_peo, n + 1)
                                v.wait_ge(a_ldr, (n + 1) * 16)
                                if n >= 2:
                                    v.wait_ge(a_str, n * 16)
                                    v.wait_ge(a_sq, n - 1)
                                v.tensor_tensor(r1b[:, (n % 2) * D:(n % 2 + 1) * D],
                                                ps_main[:, po:po + D], xr[:],
                                                op=ALU.add).then_inc(a_vt, 1)
                                v.nop().then_inc(a_pfo, 1)
                                r1_ = r1b[:, (n % 2) * D:(n % 2 + 1) * D]
                                v.reduce_sum(stat[:, n:n + 1], r1_, axis=AX.X)
                                v.reduce_max(xmaxr[:, n:n + 1], r1_, axis=AX.X)

        r_ld = sem("r_ld"); r_lg = sem("r_lg"); r_q = sem("r_q")
        r_pe = sem("r_pe"); r_ev = sem("r_ev"); r_vx = sem("r_vx")
        r_cc = sem("r_cc"); r_sq = sem("r_sq")
        wsl = [sem(f"wsl{k}") for k in range(4)]
        rsl = [sem(f"rsl{k}") for k in range(4)]
        x_q = sem("x_q"); x_pe = sem("x_pe"); x_ev = sem("x_ev")
        m_pe = sem("m_pe"); m_ev = sem("m_ev"); m_vr = sem("m_vr")
        msl = [sem(f"msl{k}") for k in range(4)]

        if not _on("f1"):
            return nc
        with ExitStack() as scopeF:
            SF = lambda n, s, d: scopeF.enter_context(nc.sbuf_tensor(n, s, d))
            w1qt = SF("w1qt", [128, 8 * FF], BF16)
            xqfm = SF("xqfm", [128, 8 * T], BF16)

            with ExitStack() as scopeF1:
                SF1 = lambda n, s, d: scopeF1.enter_context(nc.sbuf_tensor(n, s, d))
                w1c = SF1("w1c", [128, 4 * D], F32)
                w1i = SF1("w1i", [128, D], I32)
                w1b = SF1("w1b", [128, 2 * D], BF16)
                ps_w = scopeF1.enter_context(
                    nc.psum_tensor("ps_w", [128, 8 * D], BF16))

                with nc.Block() as blk:
                    @blk.sync
                    def _(s):
                        for fb in range(32):
                            if fb >= 4:
                                s.wait_ge(r_q, fb - 3)
                            s.dma_start(out=w1c[:, (fb % 4) * D:(fb % 4 + 1) * D],
                                        in_=w1_d.ap()[fb * 128:(fb + 1) * 128, :]
                                        ).then_inc(wsl[fb % 4], 16)

                    @blk.gpsimd
                    def _(g):
                        g.wait_ge(r_vx, 2)
                        g.dma_start(out=pm_d[:], in_=xm1[:, 0]).then_inc(r_lg, 16)
                        g.wait_ge(r_lg, 16)
                        g.dma_start(out=redrow[:], in_=pm_d[None, :]).then_inc(r_lg, 16)
                        g.wait_ge(r_vx, 3)
                        g.dma_start(out=cc1_in[:], in_=redsc[0, :]).then_inc(r_lg, 16)
                        g.wait_ge(r_lg, 48)
                        g.collective_compute(
                            "AllReduce", ALU.max,
                            replica_groups=[list(range(8))],
                            ins=[cc1_in.ap()], outs=[cc1_out.ap()],
                        ).then_inc(r_cc, 1)
                        g.wait_ge(r_cc, 1)
                        g.dma_start(out=gmax1[:],
                                    in_=cc1_out.ap()[None, :].broadcast_to((128, 1))
                                    ).then_inc(r_lg, 16)

                    @blk.vector
                    def _(v):
                        v.drain()
                        v.tensor_scalar(nmean[:], stat[:], -1.0 / D, None, op0=ALU.mult)
                        v.tensor_scalar(stat2[:], stat2[:], 1.0 / D, None, op0=ALU.mult)
                        v.drain()
                        v.tensor_tensor(lnsc[:], nmean[:], nmean[:], op=ALU.mult)
                        v.drain()
                        v.tensor_tensor(stat2[:], stat2[:], lnsc[:], op=ALU.subtract)
                        v.drain()
                        v.tensor_scalar(stat2[:], stat2[:], EPS_LN, None, op0=ALU.add)
                        v.drain()
                        v.reciprocal(stat2[:], stat2[:])
                        v.drain()
                        v.nop().then_inc(r_vx, 1)
                        v.wait_ge(r_sq, 1)
                        v.tensor_tensor(lnsc[:], xmaxr[:], nmean[:], op=ALU.add)
                        v.drain()
                        v.tensor_tensor(lnsc[:], lnsc[:], rstd[:], op=ALU.mult)
                        v.drain()
                        v.reduce_max(xm1[:], lnsc[:], axis=AX.X)
                        v.drain()
                        v.nop().then_inc(r_vx, 1)

                        def w1red(fb):
                            v.wait_ge(wsl[fb % 4], 16 * (fb // 4 + 1))
                            v.tensor_reduce(am1[:, fb:fb + 1],
                                            w1c[:, (fb % 4) * D:(fb % 4 + 1) * D],
                                            axis=AX.X, op=ALU.max,
                                            apply_absolute_value=True)

                        def w1q(fb):
                            v.tensor_scalar(am1[:, fb:fb + 1], am1[:, fb:fb + 1],
                                            EPS_Q, None, op0=ALU.max)
                            v.drain()
                            v.reciprocal(rc1[:, fb:fb + 1], am1[:, fb:fb + 1])
                            v.drain()
                            v.tensor_scalar(w1i[:], w1c[:, (fb % 4) * D:(fb % 4 + 1) * D],
                                            rc1[:, fb:fb + 1], 127.0,
                                            op0=ALU.mult, op1=ALU.mult)
                            if fb >= 2:
                                v.wait_ge(r_pe, fb - 1)
                            v.tensor_copy(w1b[:, (fb % 2) * D:(fb % 2 + 1) * D],
                                          w1i[:]).then_inc(r_q, 1)
                        w1red(0)
                        for fb in range(8):
                            w1red(fb + 1)
                            w1q(fb)
                        v.wait_ge(r_lg, 32)
                        v.reduce_max(redsc[:], redrow[:], axis=AX.X)
                        v.drain()
                        v.nop().then_inc(r_vx, 1)
                        for fb in range(8, 32):
                            if fb + 1 < 32:
                                w1red(fb + 1)
                            w1q(fb)
                        v.wait_ge(r_lg, 64)
                        v.tensor_scalar(gmax1[:], gmax1[:], EPS_Q, None, op0=ALU.max)
                        v.drain()
                        v.tensor_scalar(sinsc[:], gmax1[:], 1.0 / 255.0, None,
                                        op0=ALU.mult)
                        v.drain()
                        v.reciprocal(rinv1[:], sinsc[:])
                        v.tensor_scalar(s1v[:], am1[:], 1.0 / 127.0, None, op0=ALU.mult)
                        v.drain()
                        v.tensor_scalar(s1v[:], s1v[:], sinsc[:, 0:1], None, op0=ALU.mult)
                        v.drain()
                        v.reciprocal(lnsc[:], s1v[:, 0:16])
                        v.drain()
                        v.reciprocal(hrm4[:, 0:16], s1v[:, 16:32])
                        v.drain()
                        v.tensor_tensor(hsc[:, 0:16], b1t[:, 0:16], lnsc[:],
                                        op=ALU.mult)
                        v.tensor_tensor(hsc[:, 16:32], b1t[:, 16:32], hrm4[:, 0:16],
                                        op=ALU.mult)
                        v.drain()
                        v.tensor_copy(ivec[:], hsc[:])
                        v.drain()
                        v.tensor_copy(b1i[:], ivec[:])
                        v.drain()
                        v.nop().then_inc(r_vx, 1)

                    @blk.scalar
                    def _(s):
                        s.wait_ge(r_vx, 1)
                        s.sqrt(rstd[:], stat2[:]).then_inc(r_sq, 1)
                        for G in range(4):
                            s.wait_ge(r_pe, (G + 1) * 8)
                            for d in range(8):
                                s.copy(out=w1qt[:, d * FF + G * D:d * FF + (G + 1) * D],
                                       in_=ps_w[:, d * D:(d + 1) * D]).then_inc(r_ev, 1)

                    @blk.tensor
                    def _(t):
                        for fb in range(32):
                            G = fb // 8
                            t.wait_ge(r_q, fb + 1)
                            if G > 0 and fb % 8 == 0:
                                t.wait_ge(r_ev, G * 8)
                            for d in range(8):
                                ins = t.transpose(
                                    ps_w[:, d * D + (fb % 8) * 128:
                                         d * D + (fb % 8) * 128 + 128],
                                    w1b[:, (fb % 2) * D + d * 128:
                                        (fb % 2) * D + (d + 1) * 128],
                                    idb[:])
                            ins.then_inc(r_pe, 1)

            if not _on("f2a"):
                return nc
            with ExitStack() as scopeF2:
                SF2 = lambda n, s, d: scopeF2.enter_context(nc.sbuf_tensor(n, s, d))
                r1rl = SF2("r1rl", [128, 4 * D], F32)
                xqf = SF2("xqf", [128, 2 * D], F32)
                xqi = SF2("xqi", [128, 2 * D], I32)
                xqb = SF2("xqb", [128, 2 * D], BF16)

                with ExitStack() as pctx:
                    ps_w2 = pctx.enter_context(
                        nc.psum_tensor("ps_w2", [128, 8 * D], BF16))

                    with nc.Block() as blk:
                        @blk.sync
                        def _(s):
                            for t_ in range(NT):
                                if t_ < 4:
                                    s.wait_ge(r_q, 29 + t_)
                                else:
                                    s.wait_ge(x_q, t_ - 3)
                                s.dma_start(out=r1rl[:, (t_ % 4) * D:(t_ % 4 + 1) * D],
                                            in_=r1_d.ap()[t_ * 128:(t_ + 1) * 128, :]
                                            ).then_inc(rsl[t_ % 4], 16)

                        @blk.vector
                        def _(v):
                            for t_ in range(NT):
                                p = (t_ % 2) * D
                                v.wait_ge(rsl[t_ % 4], 16 * (t_ // 4 + 1))
                                src = r1rl[:, (t_ % 4) * D:(t_ % 4 + 1) * D]
                                v.tensor_scalar(xqf[:, p:p + D], src, nmean[:, t_:t_ + 1],
                                                rstd[:, t_:t_ + 1],
                                                op0=ALU.add, op1=ALU.mult)
                                v.tensor_scalar(xqf[:, p:p + D], xqf[:, p:p + D],
                                                rinv1[:, 0:1], 0.0,
                                                op0=ALU.mult, op1=ALU.max)
                                v.tensor_copy(xqi[:, p:p + D], xqf[:, p:p + D])
                                if t_ >= 2:
                                    v.wait_ge(x_pe, t_ - 1)
                                v.tensor_copy(xqb[:, p:p + D],
                                              xqi[:, p:p + D]).then_inc(x_q, 1)

                        @blk.tensor
                        def _(t):
                            for t_ in range(NT):
                                G = t_ // 8
                                t.wait_ge(x_q, t_ + 1)
                                if G > 0 and t_ % 8 == 0:
                                    t.wait_ge(x_ev, G * 8)
                                for d in range(8):
                                    ins = t.transpose(
                                        ps_w2[:, d * D + (t_ % 8) * 128:
                                              d * D + (t_ % 8) * 128 + 128],
                                        xqb[:, (t_ % 2) * D + d * 128:
                                            (t_ % 2) * D + (d + 1) * 128],
                                        idb[:])
                                ins.then_inc(x_pe, 1)

                        @blk.scalar
                        def _(s):
                            for G in range(2):
                                s.wait_ge(x_pe, (G + 1) * 8)
                                for d in range(8):
                                    s.copy(out=xqfm[:, d * T + G * D:d * T + (G + 1) * D],
                                           in_=ps_w2[:, d * D:(d + 1) * D]
                                           ).then_inc(x_ev, 1)

                if not _on("f2b"):
                    return nc
                with ExitStack() as pctx:
                    ps_mm1 = pctx.enter_context(
                        nc.psum_tensor("ps_mm1", [128, 4 * HALF], F32))
                    htb = pctx.enter_context(
                        nc.sbuf_tensor("htb", [128, 4 * HALF], F32))

                    with nc.Block() as blk:
                        @blk.tensor
                        def _(t):
                            for tt in range(4):
                                for fb in range(32):
                                    i = tt * 32 + fb
                                    if fb == 0:
                                        t.wait_ge(x_ev, 8 if tt < 2 else 16)
                                    if i >= 4:
                                        t.wait_ge(m_ev, i - 3)
                                    for d in range(8):
                                        ins = t.matmul(
                                            ps_mm1[:, (i % 4) * HALF:(i % 4 + 1) * HALF],
                                            w1qt[:, d * FF + fb * 128:
                                                 d * FF + fb * 128 + 128],
                                            xqfm[:, d * T + tt * 512:d * T + (tt + 1) * 512],
                                            start=(d == 0), stop=(d == 7))
                                    ins.then_inc(m_pe, 1)

                        @blk.scalar
                        def _(s):
                            for i in range(128):
                                fb = i % 32
                                s.wait_ge(m_pe, i + 1)
                                if i >= 4:
                                    s.wait_ge(m_vr, i - 3)
                                    s.wait_ge(msl[i % 4], 16 * (i // 4))
                                s.activation(out=htb[:, (i % 4) * HALF:(i % 4 + 1) * HALF],
                                             in_=ps_mm1[:, (i % 4) * HALF:(i % 4 + 1) * HALF],
                                             func=AF.Relu, bias=b1i[:, fb:fb + 1],
                                             scale=1.0).then_inc(m_ev, 1)

                        @blk.vector
                        def _(v):
                            for i in range(128):
                                tt, fb = divmod(i, 32)
                                v.wait_ge(m_ev, i + 1)
                                v.reduce_max(hrm4[:, fb * 4 + tt:fb * 4 + tt + 1],
                                             htb[:, (i % 4) * HALF:(i % 4 + 1) * HALF],
                                             axis=AX.X).then_inc(m_vr, 1)

                        @blk.gpsimd
                        def _(g):
                            for i in range(128):
                                tt, fb = divmod(i, 32)
                                g.wait_ge(m_ev, i + 1)
                                g.dma_start(
                                    out=h_d.ap()[tt, fb * 128:(fb + 1) * 128, :],
                                    in_=htb[:, (i % 4) * HALF:(i % 4 + 1) * HALF]
                                ).then_inc(msl[i % 4], 16)
                            for k in range(4):
                                g.wait_ge(msl[k], 16 * 32)

        h_vx = sem("h_vx"); h_lg = sem("h_lg"); h_cc = sem("h_cc")
        v_q = sem("v_q"); tp2 = sem("tp2"); te2 = sem("te2")
        w2l = [sem(f"w2l{k}") for k in range(4)]
        q_ld = sem("q_ld"); q_v = sem("q_v"); y_pe = sem("y_pe")
        y_ev = sem("y_ev"); yt_pe = sem("yt_pe"); yt_fr = sem("yt_fr")
        o_ld = sem("o_ld"); o_v = sem("o_v"); o_sq2 = sem("o_sq2")
        o_rec = sem("o_rec"); o_rsq = sem("o_rsq"); o_vv = sem("o_vv")
        o_st = sem("o_st"); dbg_q = sem("dbg_q"); dbg_y = sem("dbg_y")
        qh = [sem(f"qh{k}") for k in range(4)]
        osl = [sem(f"osl{k}") for k in range(4)]

        if not _on("g1"):
            return nc
        with ExitStack() as scopeG:
            SG = lambda n, s, d: scopeG.enter_context(nc.sbuf_tensor(n, s, d))
            w2qt = SG("w2qt", [128, 32 * D], BF16)
            hstg = SG("hstg", [128, 3 * HALF], F32)
            hqq = SG("hqq", [128, 2 * 32 * HALF], BF16)

            with ExitStack() as scopeG1:
                SG1 = lambda n, s, d: scopeG1.enter_context(nc.sbuf_tensor(n, s, d))
                w2c = SG1("w2c", [128, 2 * FF], F32)
                w2i = SG1("w2i", [128, 2048], I32)
                w2b = SG1("w2b", [128, 2 * FF], BF16)
                ps_w3 = scopeG1.enter_context(
                    nc.psum_tensor("ps_w3", [128, 2 * 8 * 128], BF16))

                with nc.Block() as blk:
                    @blk.sync
                    def _(s):
                        for j in range(16):
                            dc, h = divmod(j, 2)
                            if j == 0:
                                s.wait_ge(m_pe, 128)
                            elif j >= 4:
                                s.wait_ge(v_q, j - 3)
                            s.dma_start(
                                out=w2c[:, (j % 4) * 2048:(j % 4 + 1) * 2048],
                                in_=w2_d.ap()[dc * 128:(dc + 1) * 128,
                                              h * 2048:(h + 1) * 2048]
                            ).then_inc(w2l[j % 4], 16)

                    @blk.vector
                    def _(v):
                        v.wait_ge(m_vr, 128)
                        v.drain()
                        for fb in range(32):
                            v.reduce_max(hrm[:, fb:fb + 1],
                                         hrm4[:, fb * 4:(fb + 1) * 4], axis=AX.X)
                        v.drain()
                        v.tensor_tensor(hsc[:], hrm[:], s1v[:], op=ALU.mult)
                        v.drain()
                        v.reduce_max(hm1[:], hsc[:], axis=AX.X)
                        v.drain()
                        v.nop().then_inc(h_vx, 1)
                        v.wait_ge(h_lg, 32)
                        v.reduce_max(redsc[:], redrow[:], axis=AX.X)
                        v.drain()
                        v.nop().then_inc(h_vx, 1)

                        def w2red(dc, h):
                            j = dc * 2 + h
                            v.wait_ge(w2l[j % 4], 16 * (j // 4 + 1))
                            v.tensor_reduce(am2h[:, (dc % 2) * 2 + h:(dc % 2) * 2 + h + 1],
                                            w2c[:, (j % 4) * 2048:(j % 4 + 1) * 2048],
                                            axis=AX.X, op=ALU.max,
                                            apply_absolute_value=True)
                        w2red(0, 0)
                        w2red(0, 1)
                        v.drain()
                        for dc in range(8):
                            c2 = (dc % 2) * 2
                            if dc + 1 < 8:
                                w2red(dc + 1, 0)
                            v.tensor_tensor(am2[:, dc:dc + 1], am2h[:, c2:c2 + 1],
                                            am2h[:, c2 + 1:c2 + 2], op=ALU.max)
                            v.drain()
                            v.tensor_scalar(am2[:, dc:dc + 1], am2[:, dc:dc + 1],
                                            EPS_Q, None, op0=ALU.max)
                            v.drain()
                            v.reciprocal(rc2[:, dc:dc + 1], am2[:, dc:dc + 1])
                            v.drain()
                            v.tensor_scalar(w2i[:], w2c[:, (dc * 2 % 4) * 2048:
                                                        (dc * 2 % 4 + 1) * 2048],
                                            rc2[:, dc:dc + 1], 127.0,
                                            op0=ALU.mult, op1=ALU.mult)
                            if dc >= 2:
                                v.wait_ge(tp2, (dc - 1) * 4)
                            v.tensor_copy(w2b[:, (dc % 2) * FF:(dc % 2) * FF + 2048],
                                          w2i[:]).then_inc(v_q, 1)
                            if dc + 1 < 8:
                                w2red(dc + 1, 1)
                            else:
                                v.drain()
                            v.tensor_scalar(w2i[:], w2c[:, ((dc * 2 + 1) % 4) * 2048:
                                                        ((dc * 2 + 1) % 4 + 1) * 2048],
                                            rc2[:, dc:dc + 1], 127.0,
                                            op0=ALU.mult, op1=ALU.mult)
                            v.tensor_copy(w2b[:, (dc % 2) * FF + 2048:(dc % 2 + 1) * FF],
                                          w2i[:]).then_inc(v_q, 1)
                        v.wait_ge(h_lg, 64)
                        v.tensor_scalar(gmax2[:], gmax2[:], EPS_Q, None, op0=ALU.max)
                        v.drain()
                        v.tensor_scalar(shsc[:], gmax2[:], 1.0 / 255.0, None,
                                        op0=ALU.mult)
                        v.drain()
                        v.reciprocal(hrec[:], shsc[:])
                        v.drain()
                        v.tensor_scalar(qv2[:], s1v[:], hrec[:, 0:1], None, op0=ALU.mult)
                        v.tensor_scalar(s2v[:], am2[:], 1.0 / 127.0, None, op0=ALU.mult)
                        v.drain()
                        v.tensor_scalar(s2v[:], s2v[:], shsc[:, 0:1], None, op0=ALU.mult)
                        v.drain()
                        v.reciprocal(lnsc[:, 0:8], s2v[:])
                        v.drain()
                        v.tensor_tensor(hsc[:, 0:8], b2t[:], lnsc[:, 0:8],
                                        op=ALU.mult)
                        v.drain()
                        v.tensor_copy(ivec[:, 0:8], hsc[:, 0:8])
                        v.drain()
                        v.tensor_copy(b2i[:], ivec[:, 0:8])
                        v.drain()
                        v.nop().then_inc(h_vx, 1)

                    @blk.gpsimd
                    def _(g):
                        g.wait_ge(h_vx, 1)
                        g.dma_start(out=pm_d[:], in_=hm1[:, 0]).then_inc(h_lg, 16)
                        g.wait_ge(h_lg, 16)
                        g.dma_start(out=redrow[:], in_=pm_d[None, :]).then_inc(h_lg, 16)
                        g.wait_ge(h_vx, 2)
                        g.dma_start(out=cc2_in[:], in_=redsc[0, :]).then_inc(h_lg, 16)
                        g.wait_ge(h_lg, 48)
                        g.collective_compute(
                            "AllReduce", ALU.max,
                            replica_groups=[list(range(8))],
                            ins=[cc2_in.ap()], outs=[cc2_out.ap()],
                        ).then_inc(h_cc, 1)
                        g.wait_ge(h_cc, 1)
                        g.dma_start(out=gmax2[:],
                                    in_=cc2_out.ap()[None, :].broadcast_to((128, 1))
                                    ).then_inc(h_lg, 16)

                    @blk.tensor
                    def _(t):
                        for dc in range(8):
                            t.wait_ge(v_q, 2 * (dc + 1))
                            for g8 in range(4):
                                sg = dc * 4 + g8
                                if sg >= 2:
                                    t.wait_ge(te2, (sg - 1) * 8)
                                for fs_r in range(8):
                                    fs = g8 * 8 + fs_r
                                    ins = t.transpose(
                                        ps_w3[:, (sg % 2) * 1024 + fs_r * 128:
                                              (sg % 2) * 1024 + (fs_r + 1) * 128],
                                        w2b[:, (dc % 2) * FF + fs * 128:
                                            (dc % 2) * FF + fs * 128 + 128],
                                        idb[:])
                                ins.then_inc(tp2, 1)

                    @blk.scalar
                    def _(s):
                        for dc in range(8):
                            for g8 in range(4):
                                sg = dc * 4 + g8
                                s.wait_ge(tp2, sg + 1)
                                for fs_r in range(8):
                                    fs = g8 * 8 + fs_r
                                    s.copy(out=w2qt[:, fs * D + dc * 128:
                                                    fs * D + (dc + 1) * 128],
                                           in_=ps_w3[:, (sg % 2) * 1024 + fs_r * 128:
                                                     (sg % 2) * 1024 + (fs_r + 1) * 128]
                                           ).then_inc(te2, 1)

            if not _on("g2"):
                return nc
            with ExitStack() as scopeG2:
                SG2 = lambda n, s, d: scopeG2.enter_context(nc.sbuf_tensor(n, s, d))
                hqi = SG2("hqi", [128, 2 * HALF], I32)
                yf = SG2("yf", [128, 8 * HALF], F32)
                r1m = SG2("r1m", [128, 2 * D], F32)
                r1nm = SG2("r1nm", [128, D], F32)
                r2b = SG2("r2b", [128, 4 * D], F32)
                ps_y = scopeG2.enter_context(
                    nc.psum_tensor("ps_y", [128, 4 * HALF], F32))
                ps_yt = scopeG2.enter_context(
                    nc.psum_tensor("ps_yt", [128, 2 * D], F32))

                with nc.Block() as blk:
                    @blk.sync
                    def _(s):
                        for k in range(4):
                            s.wait_ge(msl[k], 16 * 32)
                        for idx in range(128):
                            tt, fb = divmod(idx, 32)
                            if idx >= 3:
                                s.wait_ge(q_v, idx - 2)
                            s.dma_start(
                                out=hstg[:, (idx % 3) * HALF:(idx % 3 + 1) * HALF],
                                in_=h_d.ap()[tt, fb * 128:(fb + 1) * 128, :]
                            ).then_inc(qh[idx % 3], 16)
                        s.wait_ge(o_st, 1)

                    @blk.gpsimd
                    def _(g):
                        for t_ in range(NT):
                            g.wait_ge(o_vv, t_ + 1)
                            g.dma_start(out=out_d.ap()[t_ * 128:(t_ + 1) * 128, :],
                                        in_=r2b[:, (t_ % 4) * D:(t_ % 4 + 1) * D]
                                        ).then_inc(osl[t_ % 4], 16)
                        for k in range(4):
                            g.wait_ge(osl[k], 16 * 4)
                        g.nop().then_inc(o_st, 1)

                    @blk.vector
                    def _(v):
                        def quant_fb(tt, fb):
                            if fb == 0:
                                if tt >= 2:
                                    v.wait_ge(y_pe, (tt - 1) * 8)
                                    if DEBUG:
                                        v.wait_ge(dbg_q, 16 * (tt - 1))
                            hq0 = (tt % 2) * 32 * HALF
                            idx = tt * 32 + fb
                            q = (fb % 2) * HALF
                            v.wait_ge(qh[idx % 3], 16 * (idx // 3 + 1))
                            v.tensor_scalar(hqi[:, q:q + HALF],
                                            hstg[:, (idx % 3) * HALF:(idx % 3 + 1) * HALF],
                                            qv2[:, fb:fb + 1], None, op0=ALU.mult)
                            v.tensor_copy(hqq[:, hq0 + fb * 512:hq0 + (fb + 1) * 512],
                                          hqi[:, q:q + HALF]).then_inc(q_v, 1)

                        def quant(tt):
                            for fb in range(32):
                                quant_fb(tt, fb)

                        def yf_ev(tt, db):
                            i = tt * 8 + db
                            v.wait_ge(y_pe, i + 1)
                            if DEBUG and tt >= 1 and db == 0:
                                v.wait_ge(dbg_y, 16 * tt)
                            v.tensor_scalar(yf[:, db * HALF:(db + 1) * HALF],
                                            ps_y[:, (i % 4) * HALF:(i % 4 + 1) * HALF],
                                            b2i[:, db:db + 1], s2v[:, db:db + 1],
                                            op0=ALU.add, op1=ALU.mult).then_inc(y_ev, 1)

                        def posttt(tt):
                            for ts4 in range(4):
                                t_ = tt * 4 + ts4
                                v.wait_ge(yt_pe, t_ + 1)
                                v.wait_ge(o_ld, 16 * min(NT, t_ + 2))
                                if t_ >= 4:
                                    v.wait_ge(osl[t_ % 4], 16 * (t_ // 4))
                                src = r1m[:, (t_ % 2) * D:(t_ % 2 + 1) * D]
                                v.tensor_scalar(r1nm[:], src, nmean[:, t_:t_ + 1],
                                                rstd[:, t_:t_ + 1],
                                                op0=ALU.add, op1=ALU.mult)
                                v.tensor_tensor(r2b[:, ts4 * D:(ts4 + 1) * D],
                                                ps_yt[:, (t_ % 2) * D:(t_ % 2 + 1) * D],
                                                r1nm[:], op=ALU.add)
                                v.drain()
                                v.nop().then_inc(yt_fr, 1)
                                v.nop().then_inc(o_v, 1)
                                v.reduce_sum(statb[:, t_:t_ + 1],
                                             r2b[:, ts4 * D:(ts4 + 1) * D], axis=AX.X)
                            v.wait_ge(o_sq2, (tt + 1) * 4)
                            v.drain()
                            c0, c1 = tt * 4, (tt + 1) * 4
                            v.tensor_scalar(nmeanb[:, c0:c1], statb[:, c0:c1],
                                            -1.0 / D, None, op0=ALU.mult)
                            v.tensor_scalar(stat2b[:, c0:c1], stat2b[:, c0:c1],
                                            1.0 / D, None, op0=ALU.mult)
                            v.drain()
                            v.tensor_tensor(lnsc[:, c0:c1], nmeanb[:, c0:c1],
                                            nmeanb[:, c0:c1], op=ALU.mult)
                            v.drain()
                            v.tensor_tensor(stat2b[:, c0:c1], stat2b[:, c0:c1],
                                            lnsc[:, c0:c1], op=ALU.subtract)
                            v.drain()
                            v.tensor_scalar(stat2b[:, c0:c1], stat2b[:, c0:c1],
                                            EPS_LN, None, op0=ALU.add)
                            v.drain()
                            v.reciprocal(stat2b[:, c0:c1], stat2b[:, c0:c1])
                            v.drain()
                            v.nop().then_inc(o_rec, 1)
                            v.wait_ge(o_rsq, tt + 1)
                            for ts4 in range(4):
                                t_ = tt * 4 + ts4
                                v.tensor_scalar(r2b[:, ts4 * D:(ts4 + 1) * D],
                                                r2b[:, ts4 * D:(ts4 + 1) * D],
                                                nmeanb[:, t_:t_ + 1],
                                                rstdb[:, t_:t_ + 1],
                                                op0=ALU.add, op1=ALU.mult
                                                ).then_inc(o_vv, 1)

                        quant(0)
                        for tt in range(4):
                            for db in range(8):
                                if tt + 1 < 4:
                                    for k in range(4):
                                        quant_fb(tt + 1, 4 * db + k)
                                yf_ev(tt, db)
                            posttt(tt)

                    @blk.scalar
                    def _(s):
                        s.dma_start(out=r1m[:, 0:D],
                                    in_=r1_d.ap()[0:128, :]).then_inc(o_ld, 16)
                        s.dma_start(out=r1m[:, D:2 * D],
                                    in_=r1_d.ap()[128:256, :]).then_inc(o_ld, 16)
                        for tt in range(4):
                            if DEBUG:
                                s.wait_ge(q_v, (tt + 1) * 32)
                                s.dma_start(out=hqq_dbg[tt],
                                            in_=hqq[:, (tt % 2) * 32 * HALF:
                                                    (tt % 2 + 1) * 32 * HALF]
                                            ).then_inc(dbg_q, 16)
                                s.wait_ge(y_ev, (tt + 1) * 8)
                                s.dma_start(out=yf_dbg[tt], in_=yf[:]
                                            ).then_inc(dbg_y, 16)
                            for ts4 in range(4):
                                t_ = tt * 4 + ts4
                                s.wait_ge(o_v, t_ + 1)
                                s.activation(out=sq_scr[:],
                                             in_=r2b[:, ts4 * D:(ts4 + 1) * D],
                                             func=AF.Square,
                                             accum_out=stat2b[:, t_:t_ + 1]
                                             ).then_inc(o_sq2, 1)
                                if t_ + 2 < NT:
                                    tn = t_ + 2
                                    s.dma_start(out=r1m[:, (tn % 2) * D:(tn % 2 + 1) * D],
                                                in_=r1_d.ap()[tn * 128:(tn + 1) * 128, :]
                                                ).then_inc(o_ld, 16)
                            s.wait_ge(o_rec, tt + 1)
                            s.sqrt(rstdb[:, tt * 4:(tt + 1) * 4],
                                   stat2b[:, tt * 4:(tt + 1) * 4]).then_inc(o_rsq, 1)

                    @blk.tensor
                    def _(t):
                        for tt in range(4):
                            t.wait_ge(q_v, (tt + 1) * 32)
                            hq0 = (tt % 2) * 32 * HALF
                            for db in range(8):
                                i = tt * 8 + db
                                if i >= 4:
                                    t.wait_ge(y_ev, i - 3)
                                for fs in range(32):
                                    ins = t.matmul(
                                        ps_y[:, (i % 4) * HALF:(i % 4 + 1) * HALF],
                                        w2qt[:, fs * D + db * 128:fs * D + db * 128 + 128],
                                        hqq[:, hq0 + fs * HALF:hq0 + (fs + 1) * HALF],
                                        start=(fs == 0), stop=(fs == 31))
                                ins.then_inc(y_pe, 1)
                            t.wait_ge(y_ev, (tt + 1) * 8)
                            for ts4 in range(4):
                                t_ = tt * 4 + ts4
                                if t_ >= 2:
                                    t.wait_ge(yt_fr, t_ - 1)
                                for db in range(8):
                                    ins = t.transpose(
                                        ps_yt[:, (t_ % 2) * D + db * 128:
                                              (t_ % 2) * D + db * 128 + 128],
                                        yf[:, db * HALF + ts4 * 128:
                                           db * HALF + (ts4 + 1) * 128],
                                        idf[:])
                                ins.then_inc(yt_pe, 1)

        if DEBUG:
            d_st = sem("d_st")
            with nc.Block() as blk:
                @blk.sync
                def _(s):
                    s.dma_start(out=st_dbg[:, 0:16], in_=stat[:]).then_inc(d_st, 16)
                    s.dma_start(out=st_dbg[:, 16:32], in_=nmean[:]).then_inc(d_st, 16)
                    s.dma_start(out=st_dbg[:, 32:48], in_=rstd[:]).then_inc(d_st, 16)
                    s.dma_start(out=st_dbg[:, 48:64], in_=xmaxr[:]).then_inc(d_st, 16)
                    s.wait_ge(d_st, 64)

    return nc


def _host_tables(seg_start):
    j = np.arange(HALF, dtype=np.float64)
    scale_vec = (2.0 * j + 0.4 * D) / (1.4 * D)
    inv_freq = 1.0 / 10000.0 ** (j / HALF)
    c = seg_start + 512.0
    lg = np.log(np.float64(GAMMA))

    l_own = np.arange(T, dtype=np.float64)
    p_own = seg_start + l_own
    theta = inv_freq[:, None] * p_own[None, :]
    aq = scale_vec[:, None] ** ((p_own[None, :] - c) / 512.0)
    gq = np.exp((l_own - 1024.0) * lg)
    ak = scale_vec[:, None] ** (-(p_own[None, :] - c) / 512.0)
    gk = np.exp((1024.0 - l_own) * lg)
    cq = (np.cos(theta) * aq * gq[None, :]).astype(np.float32)
    sq = (np.sin(theta) * aq * gq[None, :]).astype(np.float32)
    ck = (np.cos(theta) * ak * gk[None, :]).astype(np.float32)
    sk = (np.sin(theta) * ak * gk[None, :]).astype(np.float32)

    lw = np.arange(-WRM, 0, dtype=np.float64)
    pw = seg_start + lw
    thw = pw[:, None] * inv_freq[None, :]
    akw = scale_vec[None, :] ** (-(pw[:, None] - c) / 512.0)
    gkw = np.exp((1024.0 - lw) * lg)
    ckw = (np.cos(thw) * akw * gkw[:, None]).astype(np.float32)
    skw = (np.sin(thw) * akw * gkw[:, None]).astype(np.float32)
    return cq, sq, ck, sk, ckw, skw


def prep_in_maps(L0_attn_X, L0_attn_W_Q, L0_attn_W_K, L0_attn_W_V,
                 L0_ffn_w1, L0_ffn_b1, L0_ffn_w2, L0_ffn_b2):
    X = np.asarray(L0_attn_X, dtype=np.float32)
    perm = np.concatenate([np.arange(0, D, 2), np.arange(1, D, 2)])
    wq = np.ascontiguousarray(np.asarray(L0_attn_W_Q, np.float32)[:, perm])
    wk = np.ascontiguousarray(np.asarray(L0_attn_W_K, np.float32)[:, perm])
    wv = np.ascontiguousarray(np.asarray(L0_attn_W_V, np.float32))
    w1 = np.ascontiguousarray(np.asarray(L0_ffn_w1, np.float32))
    w2 = np.ascontiguousarray(np.asarray(L0_ffn_w2, np.float32))
    b1t = np.ascontiguousarray(np.asarray(L0_ffn_b1, np.float32).reshape(32, 128).T)
    b2t = np.ascontiguousarray(np.asarray(L0_ffn_b2, np.float32).reshape(8, 128).T)
    tri = np.triu(np.ones((128, 128))).astype(ml_dtypes.bfloat16)
    idb = np.eye(128).astype(ml_dtypes.bfloat16)
    idf = np.eye(128, dtype=np.float32)

    tables = [_host_tables(seg * T) for seg in range(4)]
    zeros_warm = np.zeros((WRM, D), np.float32)

    in_maps = []
    for core in range(8):
        seq, seg = divmod(core, 4)
        cq, sq, ck, sk, ckw, skw = tables[seg]
        xo = np.ascontiguousarray(X[seq, seg * T:(seg + 1) * T, :])
        xw = (zeros_warm if seg == 0 else
              np.ascontiguousarray(X[seq, seg * T - WRM:seg * T, :]))
        in_maps.append(dict(
            x_own=xo, x_warm=xw, wq=wq, wk=wk, wv=wv, w1=w1, w2=w2,
            b1t=b1t, b2t=b2t, cq=cq, sq=sq, ck=ck, sk=sk, ckw=ckw, skw=skw,
            tri=tri, idb=idb, idf=idf))
    return in_maps


def kernel(L0_attn_X, L0_attn_W_Q, L0_attn_W_K, L0_attn_W_V,
           L0_ffn_w1, L0_ffn_b1, L0_ffn_w2, L0_ffn_b2):
    in_maps = prep_in_maps(L0_attn_X, L0_attn_W_Q, L0_attn_W_K, L0_attn_W_V,
                           L0_ffn_w1, L0_ffn_b1, L0_ffn_w2, L0_ffn_b2)
    nc = build()
    res = run_bass_kernel_spmd(nc, in_maps, core_ids=list(range(8)))
    out = np.empty((2, 8192, D), np.float32)
    for core in range(8):
        seq, seg = divmod(core, 4)
        out[seq, seg * T:(seg + 1) * T, :] = res.results[core]["out"]
    return out
